# version 04
# baseline (speedup 1.0000x reference)
import contextlib
import numpy as np
import ml_dtypes
import concourse.bass as bass
import concourse.mybir as mybir
from concourse.bass_utils import run_bass_kernel_spmd

F32 = mybir.dt.float32
BF16 = mybir.dt.bfloat16
AF = mybir.ActivationFunctionType
ALU = mybir.AluOpType
BF = ml_dtypes.bfloat16
AX = mybir.AxisListType.X

NB, T, C, H, NH, D, M, F, E = 2, 2048, 128, 1024, 32, 32, 2, 4096, 8
NCHUNK = T // C
LN_EPS = 1e-5
DEN_EPS = 1e-6
BMIN, BMAX = 0.85, 0.9995
WIN = 256
NW = T // WIN

DEBUG = False
PHASES = "all"
STAGE = 99
NCI = 99
SUB = 99
FSTAGE = 99
FSUB = 99


def host_consts():
    c = {}
    c["identf"] = np.eye(128, dtype=np.float32)
    jj, ii = np.meshgrid(np.arange(128), np.arange(128), indexing="ij")
    c["ltri"] = (jj <= ii).astype(np.float32)
    c["tmask4"] = np.tile((jj <= ii).astype(BF), (1, 4))
    c["onescol"] = np.ones((128, 1), np.float32)
    hm = np.zeros((128, 64), np.float32)
    for p in range(128):
        hm[p, p // 32] = 1.0
    c["hmask"] = hm
    c["onesrow"] = np.ones((1, 128), np.float32)
    ee = np.zeros((64, 128), np.float32)
    for r in range(64):
        ee[r, ((r // 2) % 4) * 32:((r // 2) % 4) * 32 + 32] = 1.0
    c["econst"] = ee
    ms = np.zeros((64, 16), np.float32)
    for r in range(64):
        ms[r, 2 * ((r // 2) // 4) + (r % 2)] = 1.0
    c["msel"] = ms
    half = D // 2
    inv_freq = (1.0 / (10000.0 ** (np.arange(half, dtype=np.float32) / np.float32(half)))).astype(np.float32)
    t = np.arange(T, dtype=np.float32)
    freqs = (t[:, None] * inv_freq[None, :]).astype(np.float32)
    cos = np.cos(freqs).astype(np.float32)
    sin = np.sin(freqs).astype(np.float32)
    c["cos_t"] = cos.reshape(NCHUNK, 128, half).transpose(1, 0, 2).reshape(128, NCHUNK * half).copy()
    c["sin_t"] = sin.reshape(NCHUNK, 128, half).transpose(1, 0, 2).reshape(128, NCHUNK * half).copy()
    return c


_CONSTS = host_consts()


def make_weight_map(inputs):
    g = lambda k: np.asarray(inputs[k])
    w = {}
    w["wqT"] = np.ascontiguousarray(g("L0_attn_Wq").T)
    w["wkT"] = np.ascontiguousarray(g("L0_attn_Wk").T)
    w["wvT"] = np.ascontiguousarray(g("L0_attn_Wv").T)
    w["woutT"] = np.ascontiguousarray(g("L0_attn_Wout").T)
    w["wbmT"] = np.ascontiguousarray(np.concatenate([g("L0_attn_Wbeta"), g("L0_attn_Wmix")], 0).T)
    w["bbm_row"] = np.ascontiguousarray(np.concatenate([g("L0_attn_bbeta"), g("L0_attn_bmix")])[None, :])
    w["bout_row"] = np.ascontiguousarray(g("L0_attn_bout")[None, :])
    w["lng_row"] = np.ascontiguousarray(g("L0_attn_ln_g")[None, :])
    w["lnb_row"] = np.ascontiguousarray(g("L0_attn_ln_b")[None, :])
    w["gw1T"] = np.ascontiguousarray(g("L0_ffn_gate_w1").T)
    w["gb1c"] = np.ascontiguousarray(g("L0_ffn_gate_b1").reshape(8, 128).T)
    w["gw2T"] = np.ascontiguousarray(g("L0_ffn_gate_w2").T)
    w["gb2c"] = np.ascontiguousarray(g("L0_ffn_gate_b2").reshape(8, 1))
    w["w1T"] = np.ascontiguousarray(g("L0_ffn_W1").T)
    w["tw1T"] = np.ascontiguousarray(g("L0_ffn_TW1").transpose(0, 2, 1))
    w["b1c"] = np.ascontiguousarray(g("L0_ffn_b1").reshape(32, 128).T)
    w["tb1"] = np.ascontiguousarray(g("L0_ffn_Tb1"))
    w["w2T"] = np.ascontiguousarray(g("L0_ffn_W2").T)
    w["tw2T"] = np.ascontiguousarray(g("L0_ffn_TW2").transpose(0, 2, 1))
    w["b2row"] = np.ascontiguousarray(g("L0_ffn_b2")[None, :])
    w["tb2"] = np.ascontiguousarray(g("L0_ffn_Tb2"))
    w.update(_CONSTS)
    return w


class Bld:
    def __init__(self, nc):
        self.nc = nc
        self.dr = {}
        self.S = {}
        self.P = []
        self.sem = {}
        self.eng = None
        self.eb = None
        self.marks = {}

    def din(self, name, shape, dt=F32):
        self.dr[name] = self.nc.dram_tensor(name, list(shape), dt, kind="ExternalInput").ap()

    def dout(self, name, shape, dt=F32):
        self.dr[name] = self.nc.dram_tensor(name, list(shape), dt, kind="ExternalOutput").ap()

    def dint(self, name, shape, dt=F32):
        self.dr[name] = self.nc.dram_tensor(name, list(shape), dt, kind="Internal").ap()

    def sb(self, stack, name, shape, dt=F32):
        self.S[name] = stack.enter_context(self.nc.sbuf_tensor("s_" + name, list(shape), dt))

    DKEYS = ("dc", "dxi", "dzo", "dxo", "dou", "dzr", "dxf", "dmw", "dms", "dtb", "ddb", "dg")

    def reset(self, snap=None):
        base = dict(t=0, v=0, s=0, g=0)
        base.update({k: 0 for k in self.DKEYS})
        self.c = dict(snap) if snap else base

    SELFWAIT = ("v", "s", "g")

    def _op(self, key, which, fn, n=1):
        if key in self.SELFWAIT and self.eng == which and self.c[key] > 0:
            self.eb.wait_ge(self.sem[key], self.c[key])
        self.c[key] += n
        if self.eng == which:
            fn(self.eb).then_inc(self.sem[key], n)
        return self.c[key]

    def T(self, fn):
        return self._op("t", "tensor", fn)

    def V(self, fn):
        return self._op("v", "vector", fn)

    def Sc(self, fn):
        return self._op("s", "scalar", fn)

    def D(self, key, fn):
        return self._op(key, "sync", fn, 16)

    def DG(self, fn):
        return self._op("dg", "gpsimd", fn, 16)

    def W(self, eng, key, val):
        if self.eng == eng and val > 0:
            self.eb.wait_ge(self.sem[key], val)

    def ap(self, name, dims, off=0):
        a = self.S[name][:]
        pitch = a.ap[0][0]
        dd = [list(d) for d in dims]
        dd[0] = [pitch, dd[0][1]]
        return bass.AP(a.tensor, off, dd)

    def app(self, ps_idx, dims, off=0):
        a = self.P[ps_idx][:]
        pitch = a.ap[0][0]
        dd = [list(d) for d in dims]
        dd[0] = [pitch, dd[0][1]]
        return bass.AP(a.tensor, off, dd)


def build_program():
    nc = bass.Bass("TRN2", target_bir_lowering=False, detect_race_conditions=False)
    B = Bld(nc)
    S, P, dr, marks = B.S, B.P, B.dr, B.marks

    B.din("x", (NB, T, H))
    for n in ("wqT", "wkT", "wvT", "woutT"):
        B.din(n, (H, H))
    B.din("wbmT", (H, 128)); B.din("bbm_row", (1, 128))
    B.din("bout_row", (1, H)); B.din("lng_row", (1, H)); B.din("lnb_row", (1, H))
    B.din("gw1T", (H, H)); B.din("gb1c", (128, 8)); B.din("gw2T", (H, 8)); B.din("gb2c", (8, 1))
    B.din("w1T", (H, F)); B.din("tw1T", (E, H, F)); B.din("b1c", (128, 32)); B.din("tb1", (E, F))
    B.din("w2T", (F, H)); B.din("tw2T", (E, F, H)); B.din("b2row", (1, H)); B.din("tb2", (E, H))
    B.din("identf", (128, 128)); B.din("ltri", (128, 128)); B.din("tmask4", (128, 512), BF16)
    B.din("onescol", (128, 1)); B.din("onesrow", (1, 128)); B.din("hmask", (128, 64))
    B.din("econst", (64, 128)); B.din("msel", (64, 16))
    B.din("cos_t", (128, NCHUNK * 16)); B.din("sin_t", (128, NCHUNK * 16))

    B.dout("out", (NB, T, H))
    B.dint("x1_s", (NB, T, H))
    B.dint("zlns", (NB, 8, 128, T), BF16)
    B.dint("mw1b1", (8, 128, F), BF16)
    B.dint("mw2b1", (32, 128, H), BF16)
    if DEBUG:
        for n, sh in (("d_xln", (NB * NCHUNK, 128, H)), ("d_phiq", (NB * NCHUNK, 128, H)),
                      ("d_phik", (NB * NCHUNK, 128, H)),
                      ("d_pip", (NB * NCHUNK, 128, 128)), ("d_y", (NB * NCHUNK, 128, H)),
                      ("d_x1", (NB * NCHUNK, 128, H)), ("d_g", (8, NB)),
                      ("d_mb1", (128, 64))):
            B.dout(n, sh)
        B.dout("d_v", (NB * NCHUNK, 128, NH * 33), BF16)
        B.dout("d_u", (4, 128, WIN), BF16)
        B.dout("d_hg", (2, 128, 8 * 128), BF16)
        B.dout("d_chain", (6, 128, 64))
        B.dout("d_s8", (6, 128, 140))
        B.dout("d_gacc", (NB * NCHUNK, 8, 128))

    def plan_attn():
        for n in ("identf", "ltri", "onescol", "onesrow", "econst", "msel",
                  "cos_t", "sin_t", "bbm_row", "bout_row", "gb1c", "gb2c", "b1c",
                  "tmask4", "hmask"):
            B.D("dc", lambda e, n=n: e.dma_start(S[n][:], dr[n]))
        B.D("dc", lambda e: e.dma_start(S["lngbc"][0:1, :], dr["lng_row"]))
        cdx = B.D("dc", lambda e: e.dma_start(S["lnbbc"][0:1, :], dr["lnb_row"]))
        B.W("tensor", "dc", cdx)
        for half in range(2):
            B.T(lambda t, half=half: t.matmul(P[0][:, half * 512:(half + 1) * 512], S["onesrow"][:],
                                              S["lngbc"][0:1, half * 512:(half + 1) * 512],
                                              start=True, stop=True))
            B.T(lambda t, half=half: t.matmul(P[1][:, half * 512:(half + 1) * 512], S["onesrow"][:],
                                              S["lnbbc"][0:1, half * 512:(half + 1) * 512],
                                              start=True, stop=True))
        tb = B.c["t"]
        B.W("vector", "t", tb)
        B.V(lambda v: v.tensor_copy(S["lngbc"][:], P[0][:, 0:1024]))
        lnv = B.V(lambda v: v.tensor_copy(S["lnbbc"][:], P[1][:, 0:1024]))
        marks["lnv"] = lnv
        for (wname, dname, width) in (("wq", "wqT", H), ("wk", "wkT", H), ("wv", "wvT", H),
                                      ("wo", "woutT", H), ("wbm", "wbmT", 128),
                                      ("gw1", "gw1T", H), ("gw2", "gw2T", 8)):
            for k in range(8):
                B.DG(lambda e, wname=wname, dname=dname, width=width, k=k:
                     e.dma_start(S[wname][:, k * width:(k + 1) * width],
                                 dr[dname][k * 128:(k + 1) * 128, :]))
        marks["wdg"] = B.c["dg"]
        marks["cdx"] = cdx
        B.W("vector", "dc", cdx)
        if DEBUG:
            dv0 = B.V(lambda v: v.tensor_copy(S["dbg"][:, 0:4], S["hmask"][:, 0:4]))
            B.W("sync", "v", dv0)
            B.D("ddb", lambda e: e.dma_start(dr["d_s8"][3][:, 70:74], S["dbg"][:, 0:4]))
        B.V(lambda v: v.memset(S["Tma"][:], 0.0))
        B.V(lambda v: v.memset(S["Sbf"][:], 0.0))
        B.V(lambda v: v.memset(S["vstg"][:], 0.0))
        B.V(lambda v: v.memset(B.ap("vstg", [[1, 128], [33, NH], [1, 1]], off=32), 1.0))
        B.V(lambda v: v.memset(S["plastc"][:], 1.0))
        B.V(lambda v: v.memset(S["gacc"][:], 0.0))

        nci_done = 0
        for b in range(NB):
            for cc in range(NCHUNK):
                if nci_done < NCI:
                    attn_ci(b, cc)
                    nci_done += 1
        marks["attn_end"] = dict(B.c)

    def ln_stats_apply(src, dst):
        B.Sc(lambda s: s.activation(S["scr1"][:], S[src][:], AF.Copy, accum_out=S["stat"][:, 0:1]))
        s2 = B.Sc(lambda s: s.activation(S["scr2"][:], S[src][:], AF.Square, accum_out=S["stat"][:, 1:2]))
        B.W("vector", "s", s2)
        B.V(lambda v: v.tensor_scalar(S["stat"][:, 2:3], S["stat"][:, 0:1], 1.0 / H, None, ALU.mult))
        B.V(lambda v: v.tensor_tensor(S["stat"][:, 3:4], S["stat"][:, 2:3], S["stat"][:, 2:3], ALU.mult))
        vv = B.V(lambda v: v.tensor_scalar(S["stat"][:, 4:5], S["stat"][:, 1:2], 1.0 / H, S["stat"][:, 3:4],
                                           ALU.mult, ALU.subtract))
        vv = B.V(lambda v: v.tensor_scalar(S["stat"][:, 4:5], S["stat"][:, 4:5], LN_EPS, None, ALU.add))
        B.W("scalar", "v", vv)
        B.Sc(lambda s: s.activation(S["stat"][:, 5:6], S["stat"][:, 4:5], AF.Ln))
        sr = B.Sc(lambda s: s.activation(S["stat"][:, 6:7], S["stat"][:, 5:6], AF.Exp, scale=-0.5))
        B.W("vector", "s", sr)
        B.V(lambda v: v.tensor_tensor(S["stat"][:, 7:8], S["stat"][:, 2:3], S["stat"][:, 6:7], ALU.mult))
        return B.V(lambda v: v.tensor_scalar(S[dst][:], S[src][:], S["stat"][:, 6:7], S["stat"][:, 7:8],
                                             ALU.mult, ALU.subtract))

    def fm_transpose(src, dst):
        for fc in range(8):
            pi, poff = (4, (fc % 4) * 128) if fc < 4 else (5, (fc % 4) * 128)
            B.T(lambda t, fc=fc, pi=pi, poff=poff:
                t.transpose(P[pi][:, poff:poff + 128], S[src][:, fc * 128:(fc + 1) * 128], S["identf"][:]))
        tt = B.c["t"]
        B.W("vector", "t", tt)
        B.V(lambda v: v.tensor_copy(S[dst][:, 0:512], P[4][:, 0:512]))
        return B.V(lambda v: v.tensor_copy(S[dst][:, 512:1024], P[5][:, 0:512]))

    def attn_ci(b, cc):
        ci = b * NCHUNK + cc
        t0 = cc * C

        B.W("sync", "v", marks.get(("cidone", ci - 1), 0))
        B.W("sync", "dxo", B.c["dxo"])
        dxv = B.D("dxi", lambda e: e.dma_start(S["xt"][:], dr["x"][b, t0:t0 + C, :]))
        B.W("scalar", "dxi", dxv)
        if ci == 0:
            B.W("scalar", "v", marks["lnv"])
        xlnv = ln_stats_apply("xt", "xln")
        if DEBUG:
            B.W("sync", "v", xlnv)
            B.D("ddb", lambda e: e.dma_start(dr["d_xln"][ci], S["xln"][:]))
        if STAGE < 2:
            marks[("cidone", ci)] = B.c["v"]
            return
        B.W("tensor", "v", xlnv)
        B.W("vector", "dzo", B.c["dzo"])
        exv = fm_transpose("xln", "xlnfm")
        if STAGE < 3:
            marks[("cidone", ci)] = B.c["v"]
            return
        B.W("tensor", "v", exv)
        if ci == 0:
            B.W("tensor", "dg", marks["wdg"])
        def proj(w, oc0, width, pi, poff, wwidth=H, close=True):
            for k in range(8):
                B.T(lambda t, w=w, oc0=oc0, width=width, pi=pi, poff=poff, k=k, wwidth=wwidth, close=close:
                    t.matmul(P[pi][:, poff:poff + width],
                             S["xlnfm"][:, k * 128:(k + 1) * 128],
                             S[w][:, k * wwidth + oc0:k * wwidth + oc0 + width],
                             start=(k == 0), stop=(k == 7 and close)))
        proj("wq", 0, 512, 0, 0)
        proj("wq", 512, 512, 0, 512)
        proj("wk", 0, 512, 1, 0)
        proj("wk", 512, 512, 1, 512)
        proj("wv", 0, 512, 2, 0)
        proj("wv", 512, 512, 3, 0)
        proj("wbm", 0, 128, 4, 0, wwidth=128, close=False)
        tproj = B.T(lambda t: t.matmul(P[4][:, 0:128], S["onesrow"][:], S["bbm_row"][:],
                                       start=False, stop=True))

        if STAGE < 4:
            marks[("cidone", ci)] = B.c["v"]
            return
        B.W("vector", "t", tproj)
        co = cc * 16
        for (src_pi, dst) in ((0, "rotq"), (1, "rotk")):
            u1 = B.app(src_pi, [[1, 128], [32, 32], [1, 16]], off=0)
            u2 = B.app(src_pi, [[1, 128], [32, 32], [1, 16]], off=16)
            cosb = B.ap("cos_t", [[1, 128], [0, 32], [1, 16]], off=co)
            sinb = B.ap("sin_t", [[1, 128], [0, 32], [1, 16]], off=co)
            o1 = B.ap(dst, [[1, 128], [32, 32], [1, 16]], off=0)
            o2 = B.ap(dst, [[1, 128], [32, 32], [1, 16]], off=16)
            B.V(lambda v, u2=u2, sinb=sinb: v.tensor_tensor(S["scr1"][:, 0:512], u2, sinb, ALU.mult))
            B.V(lambda v, u1=u1, cosb=cosb: v.tensor_tensor(S["scr2"][:, 0:512], u1, cosb, ALU.mult))
            B.V(lambda v, o1=o1: v.tensor_tensor(o1, S["scr2"][:, 0:512], S["scr1"][:, 0:512], ALU.subtract))
            B.V(lambda v, u1=u1, sinb=sinb: v.tensor_tensor(S["scr1"][:, 0:512], u1, sinb, ALU.mult))
            B.V(lambda v, u2=u2, cosb=cosb: v.tensor_tensor(S["scr2"][:, 0:512], u2, cosb, ALU.mult))
            B.V(lambda v, o2=o2: v.tensor_tensor(o2, S["scr2"][:, 0:512], S["scr1"][:, 0:512], ALU.add))
        vd1 = B.ap("vstg", [[1, 128], [33, 16], [1, 32]], off=0)
        vd2 = B.ap("vstg", [[1, 128], [33, 16], [1, 32]], off=16 * 33)
        B.V(lambda v, vd1=vd1: v.tensor_copy(vd1, B.app(2, [[1, 128], [32, 16], [1, 32]], off=0)))
        ve1 = B.V(lambda v, vd2=vd2: v.tensor_copy(vd2, B.app(3, [[1, 128], [32, 16], [1, 32]], off=0)))
        if DEBUG:
            B.W("sync", "v", ve1)
            B.D("ddb", lambda e: e.dma_start(dr["d_v"][ci], S["vstg"][:]))
        B.W("scalar", "t", tproj)
        se = B.Sc(lambda s: s.activation(S["bmt"][:, 0:64], P[4][:, 0:64], AF.Exp, scale=-1.0))
        B.W("vector", "s", se)
        B.V(lambda v: v.tensor_scalar(S["bmt"][:, 0:64], S["bmt"][:, 0:64], 1.0, None, ALU.add))
        B.V(lambda v: v.reciprocal(S["scr1"][:, 0:64], S["bmt"][:, 0:64]))
        bclip = B.V(lambda v: v.tensor_scalar(S["bmt"][:, 0:64], S["scr1"][:, 0:64], BMIN, BMAX,
                                              ALU.max, ALU.min))
        B.W("scalar", "v", bclip)
        slog = B.Sc(lambda s: s.activation(S["bmt"][:, 64:128], S["bmt"][:, 0:64], AF.Ln))
        vmix = B.V(lambda v: v.tensor_copy(S["scr2"][:, 0:64], P[4][:, 64:128]))
        a1 = B.ap("scr2", [[1, 128], [2, 32], [1, 1]], off=0)
        a2 = B.ap("scr2", [[1, 128], [2, 32], [1, 1]], off=1)
        B.V(lambda v, a1=a1, a2=a2: v.tensor_tensor(S["scr1"][:, 0:32], a1, a2, ALU.subtract))
        sm = B.Sc(lambda s: s.activation(S["scr1"][:, 32:64], S["scr1"][:, 0:32], AF.Exp, scale=-1.0))
        B.W("vector", "s", sm)
        B.V(lambda v: v.tensor_scalar(S["scr1"][:, 32:64], S["scr1"][:, 32:64], 1.0, None, ALU.add))
        B.V(lambda v: v.reciprocal(S["scr2"][:, 0:32], S["scr1"][:, 32:64]))
        mx1 = B.ap("mixw", [[1, 128], [2, 32], [1, 1]], off=0)
        mx2 = B.ap("mixw", [[1, 128], [2, 32], [1, 1]], off=1)
        B.V(lambda v, mx1=mx1: v.tensor_copy(mx1, S["scr2"][:, 0:32]))
        mixv = B.V(lambda v, mx2=mx2: v.tensor_scalar(mx2, S["scr2"][:, 0:32], -1.0, 1.0, ALU.mult, ALU.add))

        if STAGE < 5:
            marks[("cidone", ci)] = B.c["v"]
            return
        B.W("tensor", "s", slog)
        B.W("tensor", "v", vmix)
        B.T(lambda t: t.matmul(P[4][:, 128:192], S["ltri"][:], S["bmt"][:, 64:128], start=True, stop=True))
        tcl = B.T(lambda t: t.matmul(P[4][0:64, 192:193], S["bmt"][:, 64:128], S["onescol"][:],
                                     start=True, stop=True))
        B.W("scalar", "t", tcl)
        B.Sc(lambda s: s.activation(S["pp"][:], P[4][:, 128:192], AF.Exp))
        sip = B.Sc(lambda s: s.activation(S["pip"][:], P[4][:, 128:192], AF.Exp, scale=-1.0))
        if DEBUG:
            B.W("vector", "s", sip)
            dv1 = B.V(lambda v: v.tensor_copy(S["scr1"][:, 0:64], S["pp"][:]))
            dv2 = B.V(lambda v: v.tensor_copy(S["scr1"][:, 64:128], S["pip"][:]))
            B.W("sync", "v", dv2)
            B.D("ddb", lambda e: e.dma_start(dr["d_pip"][ci], S["scr1"][:, 0:128]))
        B.W("vector", "t", tcl)
        clap = B.app(4, [[1, 64], [0, 16]], off=192)
        mselv = B.V(lambda v, clap=clap: v.tensor_tensor(S["mbstat"][0:64, 0:16], S["msel"][:], clap, ALU.mult))
        B.W("tensor", "v", mselv)
        B.W("tensor", "s", sip)
        tpl = B.T(lambda t: t.matmul(P[4][:, 208:224], S["econst"][:], S["mbstat"][0:64, 0:16],
                                     start=True, stop=True))
        B.W("scalar", "t", tpl)
        slot = (ci % 2) * 16
        pslot = B.Sc(lambda s, slot=slot: s.activation(S["plastc"][:, slot:slot + 16], P[4][:, 208:224], AF.Exp))
        if DEBUG and ci == 0:
            B.W("vector", "s", pslot)
            srcs = [("bmt", 0, 64), ("bmt", 64, 64), (4, 128, 64), ("pp", 0, 64),
                    ("pip", 0, 64), ("plastc", slot, 16)]
            for i6, (nm, off, wd) in enumerate(srcs):
                B.W("vector", "ddb", B.c["ddb"])
                src = P[4][:, off:off + wd] if nm == 4 else S[nm][:, off:off + wd]
                dv_ = B.V(lambda v, src=src, wd=wd: v.tensor_copy(S["dbg"][:, 0:wd], src))
                B.W("sync", "v", dv_)
                B.D("ddb", lambda e, i6=i6, wd=wd: e.dma_start(dr["d_chain"][i6][:, 0:wd], S["dbg"][:, 0:wd]))

        if STAGE < 6:
            marks[("cidone", ci)] = B.c["v"]
            return
        vminq = B.V(lambda v: v.tensor_scalar_min(S["scr1"][:], S["rotq"][:], 0.0))
        B.W("scalar", "v", vminq)
        B.Sc(lambda s: s.activation(S["scr2"][:], S["scr1"][:], AF.Exp))
        sphi = B.Sc(lambda s: s.activation(S["phiq"][:], S["rotq"][:], AF.Relu))
        B.W("vector", "s", sphi)
        B.V(lambda v: v.tensor_tensor(S["phiq"][:], S["phiq"][:], S["scr2"][:], ALU.add))
        vmink = B.V(lambda v: v.tensor_scalar_min(S["scr1"][:], S["rotk"][:], 0.0))
        B.W("scalar", "v", vmink)
        B.Sc(lambda s: s.activation(S["scr2"][:], S["scr1"][:], AF.Exp))
        sphk = B.Sc(lambda s: s.activation(S["phik"][:], S["rotk"][:], AF.Relu))
        B.W("vector", "s", sphk)
        phkv = B.V(lambda v: v.tensor_tensor(S["phik"][:], S["phik"][:], S["scr2"][:], ALU.add))
        if DEBUG:
            B.W("sync", "v", phkv)
            B.D("ddb", lambda e: e.dma_start(dr["d_phiq"][ci], S["phiq"][:]))
            B.D("ddb", lambda e: e.dma_start(dr["d_phik"][ci], S["phik"][:]))
        B.W("tensor", "v", phkv)
        for fc in range(8):
            B.T(lambda t, fc=fc: t.transpose(P[0][:, fc * 128:(fc + 1) * 128],
                                             S["phiq"][:, fc * 128:(fc + 1) * 128], S["identf"][:]))
        for fc in range(8):
            B.T(lambda t, fc=fc: t.transpose(P[1][:, fc * 128:(fc + 1) * 128],
                                             S["phik"][:, fc * 128:(fc + 1) * 128], S["identf"][:]))
        tphT = B.c["t"]
        B.W("vector", "t", tphT)
        B.W("scalar", "t", tphT)
        for hl in range(4):
            for hf in range(2):
                mb_ap = B.ap("hmask", [[1, 128], [0, 512]], off=hl)
                B.V(lambda v, hl=hl, hf=hf, mb_ap=mb_ap: v.tensor_tensor(
                    S["phiqm"][:, hl * 1024 + hf * 512:hl * 1024 + hf * 512 + 512],
                    P[0][:, hf * 512:hf * 512 + 512], mb_ap, ALU.mult))
        phfv = B.V(lambda v: v.tensor_copy(S["phikf"][:], P[1][:, 0:1024]))

        if STAGE < 7:
            marks[("cidone", ci)] = B.c["v"]
            return
        B.W("vector", "s", sip)
        for m in range(M):
            ipb = B.ap("pip", [[1, 128], [2, 32], [0, 32]], off=m)
            B.V(lambda v, m=m, ipb=ipb: v.tensor_tensor(S["ks"][:, m * H:(m + 1) * H], S["phik"][:], ipb, ALU.mult))
            vsb = B.ap("vs", [[1, 128], [66, 32], [1, 32]], off=m * 33)
            vsr = B.ap("vstg", [[1, 128], [33, 32], [1, 32]], off=0)
            B.V(lambda v, vsb=vsb, vsr=vsr, ipb=ipb: v.tensor_tensor(vsb, vsr, ipb, ALU.mult))
            ipd = B.ap("vs", [[1, 128], [66, 32], [1, 1]], off=m * 33 + 32)
            ips = B.ap("pip", [[1, 128], [2, 32], [1, 1]], off=m)
            B.V(lambda v, ipd=ipd, ips=ips: v.tensor_copy(ipd, ips))
        opsv = B.c["v"]

        if STAGE < 8:
            marks[("cidone", ci)] = B.c["v"]
            return
        B.W("tensor", "v", opsv)
        B.W("tensor", "s", pslot)
        for g8 in range(8):
            for hl in range(4):
                B.T(lambda t, g8=g8, hl=hl: t.matmul(
                    P[4][:, hl * 128:(hl + 1) * 128],
                    S["phikf"][:, g8 * 128:(g8 + 1) * 128],
                    S["phiqm"][:, hl * 1024 + g8 * 128:hl * 1024 + (g8 + 1) * 128],
                    start=True, stop=True))
            tg = B.c["t"]
            B.W("vector", "t", tg)
            gev = B.V(lambda v: v.tensor_tensor(S["gsb"][:], P[4][:, 0:512], S["tmask4"][:], ALU.mult))
            B.W("tensor", "v", gev)
            for hl in range(4):
                h = 4 * g8 + hl
                if SUB < 3:
                    continue
                B.T(lambda t, hl=hl, h=h: t.matmul(
                    P[5][:, hl * 66:hl * 66 + 66],
                    S["gsb"][:, hl * 128:(hl + 1) * 128],
                    S["vs"][:, h * 66:(h + 1) * 66],
                    start=True, stop=(SUB < 4)))
                if SUB < 4:
                    continue
                for m in range(M):
                    soff = (b * 16 + g8 * 2 + m) * 132
                    B.T(lambda t, hl=hl, m=m, soff=soff, g8=g8: t.matmul(
                        P[5][:, hl * 66 + m * 33:hl * 66 + m * 33 + 33],
                        S["phiqm"][:, hl * 1024 + g8 * 128:hl * 1024 + (g8 + 1) * 128],
                        S["Sbf"][:, soff + 33 * hl:soff + 33 * hl + 33],
                        start=False, stop=(m == M - 1)))
            tav = B.c["t"]
            for m in range(M):
                if SUB < 5:
                    continue
                B.T(lambda t, g8=g8, m=m: t.matmul(
                    P[2][:, m * 132:(m + 1) * 132],
                    S["ks"][:, m * H + g8 * 128:m * H + (g8 + 1) * 128],
                    S["vstg"][:, 4 * g8 * 33:4 * g8 * 33 + 132],
                    start=True, stop=True))
            tst = B.c["t"]
            if SUB < 6:
                continue

            B.W("vector", "t", tav)
            B.W("vector", "s", pslot)
            dra = B.app(5, [[1, 128], [66, 4], [33, 2]], off=32)
            ppg = B.ap("pp", [[1, 128], [2, 4], [1, 2]], off=8 * g8)
            B.V(lambda v, dra=dra, ppg=ppg: v.tensor_tensor(S["mbstat"][:, 0:8], dra, ppg, ALU.mult))
            B.V(lambda v: v.tensor_scalar(S["mbstat"][:, 0:8], S["mbstat"][:, 0:8], DEN_EPS, None, ALU.add))
            B.V(lambda v: v.reciprocal(S["mbstat"][:, 8:16], S["mbstat"][:, 0:8]))
            mwg = B.ap("mixw", [[1, 128], [2, 4], [1, 2]], off=8 * g8)
            B.V(lambda v, mwg=mwg, ppg=ppg: v.tensor_tensor(S["mbstat"][:, 0:8], mwg, ppg, ALU.mult))
            B.V(lambda v: v.tensor_tensor(S["mbstat"][:, 0:8], S["mbstat"][:, 0:8], S["mbstat"][:, 8:16], ALU.mult))
            for m in range(M):
                numm = B.app(5, [[1, 128], [66, 4], [1, 32]], off=m * 33)
                wmb = B.ap("mbstat", [[1, 128], [2, 4], [0, 32]], off=m)
                dst = (S["scr1"] if m == 0 else S["scr2"])[:, 0:128]
                B.V(lambda v, numm=numm, wmb=wmb, dst=dst: v.tensor_tensor(dst, numm, wmb, ALU.mult))
            B.V(lambda v, g8=g8: v.tensor_tensor(S["yt"][:, g8 * 128:(g8 + 1) * 128],
                                                 S["scr1"][:, 0:128], S["scr2"][:, 0:128], ALU.add))
            B.W("vector", "t", tst)
            pprev = ((ci + 1) % 2) * 16
            for m in range(M):
                soff = (b * 16 + g8 * 2 + m) * 132
                pcp = S["plastc"][:, pprev + 2 * g8 + m:pprev + 2 * g8 + m + 1]
                pcc = S["plastc"][:, slot + 2 * g8 + m:slot + 2 * g8 + m + 1]
                B.V(lambda v, soff=soff, pcp=pcp, m=m: v.scalar_tensor_tensor(
                    S["Tma"][:, soff:soff + 132], S["Tma"][:, soff:soff + 132], pcp,
                    P[2][:, m * 132:(m + 1) * 132], ALU.mult, ALU.add))
                B.V(lambda v, soff=soff, pcc=pcc: v.tensor_scalar(
                    S["Sbf"][:, soff:soff + 132], S["Tma"][:, soff:soff + 132], pcc, None, ALU.mult))
            stv = B.c["v"]
            B.W("tensor", "v", stv)
        yv_all = B.c["v"]
        if DEBUG and ci == 0:
            srcs2 = [("phiqm", 0, 128), ("phiqm", 1024, 128), ("gsb", 0, 128),
                     ("vs", 0, 66), ("Sbf", (b * 16) * 132, 132), ("ks", 0, 128)]
            for i6, (nm, off, wd) in enumerate(srcs2):
                B.W("vector", "ddb", B.c["ddb"])
                dv_ = B.V(lambda v, nm=nm, off=off, wd=wd: v.tensor_copy(S["dbg"][:, 0:wd], S[nm][:, off:off + wd]))
                B.W("sync", "v", dv_)
                B.D("ddb", lambda e, i6=i6, wd=wd: e.dma_start(dr["d_s8"][i6][:, 0:wd], S["dbg"][:, 0:wd]))
        if DEBUG:
            B.W("sync", "v", yv_all)
            B.D("ddb", lambda e: e.dma_start(dr["d_y"][ci], S["yt"][:]))

        if STAGE < 9:
            marks[("cidone", ci)] = B.c["v"]
            return
        B.W("tensor", "v", yv_all)
        for fc in range(8):
            B.T(lambda t, fc=fc: t.transpose(P[0][:, fc * 128:(fc + 1) * 128],
                                             S["yt"][:, fc * 128:(fc + 1) * 128], S["identf"][:]))
        tyT = B.c["t"]
        B.W("vector", "t", tyT)
        yfv = B.V(lambda v: v.tensor_copy(S["yfm"][:], P[0][:, 0:1024]))
        B.W("tensor", "v", yfv)
        for oc in (0, 512):
            for k in range(8):
                B.T(lambda t, oc=oc, k=k: t.matmul(P[1][:, oc:oc + 512],
                                                   S["yfm"][:, k * 128:(k + 1) * 128],
                                                   S["wo"][:, k * H + oc:k * H + oc + 512],
                                                   start=(k == 0), stop=False))
            B.T(lambda t, oc=oc: t.matmul(P[1][:, oc:oc + 512], S["onesrow"][:],
                                          S["bout_row"][:, oc:oc + 512], start=False, stop=True))
        two = B.c["t"]
        B.W("vector", "t", two)
        zv = B.V(lambda v: v.tensor_tensor(S["zt"][:], P[1][:, 0:1024], S["xln"][:], ALU.add))
        B.W("scalar", "v", zv)
        nzv = ln_stats_apply("zt", "scr1")
        B.V(lambda v: v.tensor_tensor(S["scr2"][:], S["scr1"][:], S["lngbc"][:], ALU.mult))
        B.V(lambda v: v.tensor_tensor(S["scr1"][:], S["scr2"][:], S["lnbbc"][:], ALU.add))
        x1v = B.V(lambda v: v.tensor_tensor(S["xt"][:], S["xt"][:], S["scr1"][:], ALU.add))
        B.W("sync", "v", x1v)
        B.D("dxo", lambda e: e.dma_start(dr["x1_s"][b, t0:t0 + C, :], S["xt"][:]))
        if PHASES == "attn":
            B.D("dou", lambda e: e.dma_start(dr["out"][b, t0:t0 + C, :], S["xt"][:]))
        if DEBUG:
            B.D("ddb", lambda e: e.dma_start(dr["d_x1"][ci], S["xt"][:]))

        if STAGE < 10:
            marks[("cidone", ci)] = B.c["v"]
            return
        B.W("scalar", "v", x1v)
        z2v = ln_stats_apply("xt", "xln")
        B.W("tensor", "v", z2v)
        ex2 = fm_transpose("xln", "xlnfm")
        B.W("sync", "v", ex2)
        for fc in range(8):
            B.D("dzo", lambda e, fc=fc: e.dma_start(dr["zlns"][b, fc, :, t0:t0 + C],
                                                    S["xlnfm"][:, fc * 128:(fc + 1) * 128]))
        B.W("tensor", "v", ex2)
        for ho in range(8):
            for k in range(8):
                B.T(lambda t, ho=ho, k=k: t.matmul(P[0][:, ho * 128:(ho + 1) * 128],
                                                   S["gw1"][:, k * H + ho * 128:k * H + ho * 128 + 128],
                                                   S["xlnfm"][:, k * 128:(k + 1) * 128],
                                                   start=(k == 0), stop=(k == 7)))
        tg1 = B.c["t"]
        B.W("scalar", "t", tg1)
        for ho in range(8):
            B.Sc(lambda s, ho=ho: s.activation(S["hg"][:, ho * 128:(ho + 1) * 128],
                                               P[0][:, ho * 128:(ho + 1) * 128], AF.Relu,
                                               bias=S["gb1c"][:, ho:ho + 1]))
        sg1 = B.c["s"]
        if DEBUG and ci in (0, 16):
            B.W("sync", "s", sg1)
            B.D("ddb", lambda e, ci=ci: e.dma_start(dr["d_hg"][ci // 16], S["hg"][:]))
        B.W("tensor", "s", sg1)
        for k in range(8):
            B.T(lambda t, k=k: t.matmul(P[4][0:8, 256:384], S["gw2"][:, k * 8:(k + 1) * 8],
                                        S["hg"][:, k * 128:(k + 1) * 128],
                                        start=(k == 0), stop=(k == 7)))
        tg2 = B.c["t"]
        B.W("vector", "t", tg2)
        B.V(lambda v, b=b: v.tensor_tensor(S["gacc"][:, b * 128:(b + 1) * 128],
                                           S["gacc"][:, b * 128:(b + 1) * 128],
                                           P[4][0:8, 256:384], ALU.add))
        if DEBUG:
            dgc = B.V(lambda v: v.tensor_copy(S["scrg"][0:8, :], P[4][0:8, 256:384]))
            B.W("sync", "v", dgc)
            B.D("ddb", lambda e, ci=ci: e.dma_start(dr["d_gacc"][ci], S["scrg"][0:8, :]))
        marks[("cidone", ci)] = B.c["v"]

    def plan_ffn():
        ae = marks["attn_end"]
        for e in ("tensor", "vector", "scalar", "gpsimd", "sync"):
            B.W(e, "t", ae["t"]); B.W(e, "v", ae["v"]); B.W(e, "s", ae["s"])
            for k in Bld.DKEYS:
                B.W(e, k, ae[k])

        dtb2 = B.D("dc", lambda e: e.dma_start(S["tb2s"][:], dr["tb2"]))
        db2r = B.D("dc", lambda e: e.dma_start(S["scrF"][0:1, :], dr["b2row"]))
        for b in range(NB):
            B.V(lambda v, b=b: v.tensor_reduce(S["gcol"][:, b:b + 1], S["gacc"][:, b * 128:(b + 1) * 128],
                                               AX, ALU.add))
            B.V(lambda v, b=b: v.tensor_scalar(S["gcol"][:, b:b + 1], S["gcol"][:, b:b + 1],
                                               1.0 / T, S["gb2c"][:], ALU.mult, ALU.add))
        gv = B.c["v"]
        if DEBUG:
            B.W("sync", "v", gv)
            B.D("ddb", lambda e: e.dma_start(dr["d_g"], S["gcol"][:]))
        if FSUB < 2:
            return
        B.W("tensor", "v", gv)
        for b in range(NB):
            B.T(lambda t, b=b: t.transpose(P[4][0:1, 240 + 8 * b:248 + 8 * b],
                                           S["gcol"][:, b:b + 1], S["identf"][0:8, 0:8]))
        tgr = B.c["t"]
        B.W("vector", "t", tgr)
        grv = B.V(lambda v: v.tensor_copy(S["grow"][0:1, 0:16], P[4][0:1, 240:256]))
        B.W("tensor", "v", grv)
        tgb = B.T(lambda t: t.matmul(P[4][:, 0:16], S["onesrow"][:], S["grow"][0:1, 0:16], start=True, stop=True))
        B.W("vector", "t", tgb)
        gbv = B.V(lambda v: v.tensor_copy(S["gbc"][:], P[4][:, 0:16]))
        if FSUB < 3:
            return
        B.W("tensor", "v", gbv)
        for fcch in range(32):
            B.W("sync", "t", marks.get(("tb1t", fcch - 1), 0))
            B.D("dtb", lambda e, fcch=fcch: e.dma_start(S["tb1st"][:], dr["tb1"][:, fcch * 128:(fcch + 1) * 128]))
            dmb = B.c["dtb"]
            B.W("tensor", "dtb", dmb)
            marks[("tb1t", fcch)] = B.T(lambda t, fcch=fcch: t.matmul(
                P[5][:, fcch * 2:fcch * 2 + 2], S["tb1st"][:], S["gcol"][:], start=True, stop=True))
        tmb1 = B.c["t"]
        B.W("vector", "t", tmb1)
        b1b = B.ap("b1c", [[1, 128], [1, 32], [0, 2]], off=0)
        B.V(lambda v, b1b=b1b: v.tensor_tensor(S["mb1c"][:], P[5][:, 0:64], b1b, ALU.add))
        if DEBUG:
            dmb1 = B.c["v"]
            B.W("sync", "v", dmb1)
            B.D("ddb", lambda e: e.dma_start(dr["d_mb1"], S["mb1c"][:]))
        if FSUB < 4:
            return
        B.W("tensor", "dc", db2r)
        B.W("vector", "dc", db2r)
        for b in range(NB):
            for half in range(2):
                B.W("tensor", "v", B.c["v"])
                tm2 = B.T(lambda t, b=b, half=half: t.matmul(
                    P[2][0:1, 0:512], S["gcol"][:, b:b + 1],
                    S["tb2s"][:, half * 512:(half + 1) * 512], start=True, stop=True))
                B.W("vector", "t", tm2)
                B.V(lambda v, b=b, half=half: v.tensor_tensor(
                    S["mb2row"][0:1, b * H + half * 512:b * H + half * 512 + 512],
                    P[2][0:1, 0:512],
                    S["scrF"][0:1, half * 512:(half + 1) * 512], ALU.add))
        mb2v = B.c["v"]

        if FSTAGE < 2:
            return
        NSLOT = 1
        stage_seq = []
        for mat in (1, 2):
            nhc, nfb = (8, 8) if mat == 1 else (32, 2)
            for hc in range(nhc):
                for fb in range(nfb):
                    stage_seq.append((mat, hc, fb))
        vdone = {}
        for si, (mat, hc, fb) in enumerate(stage_seq):
            wT, twT, dramb1 = ("w1T", "tw1T", "mw1b1") if mat == 1 else ("w2T", "tw2T", "mw2b1")
            slot = si % NSLOT
            B.W("gpsimd", "v", vdone.get(si - NSLOT, mb2v))
            B.DG(lambda e, wT=wT, hc=hc, fb=fb, slot=slot:
                 e.dma_start(S["wst"][:, slot * 512:slot * 512 + 512],
                             dr[wT][hc * 128:(hc + 1) * 128, fb * 512:(fb + 1) * 512]))
            for e8 in range(E):
                B.DG(lambda e, twT=twT, e8=e8, hc=hc, fb=fb, slot=slot:
                     e.dma_start(S["twst"][:, (slot * E + e8) * 512:(slot * E + e8) * 512 + 512],
                                 dr[twT][e8, hc * 128:(hc + 1) * 128, fb * 512:(fb + 1) * 512]))
            dall = B.c["dg"]
            B.W("vector", "dg", dall)
            for b in range(NB):
                if b == 0:
                    dsl = (S["mw1"][:, hc * F + fb * 512:hc * F + fb * 512 + 512] if mat == 1
                           else S["mw2"][:, hc * H + fb * 512:hc * H + fb * 512 + 512])
                else:
                    dsl = S["scrW"][:, 0:512]
                    B.W("vector", "dms", B.c["dms"])
                for e8 in range(E):
                    src1 = S["wst"][:, slot * 512:slot * 512 + 512] if e8 == 0 else dsl
                    B.V(lambda v, dsl=dsl, e8=e8, b=b, slot=slot, src1=src1:
                        v.scalar_tensor_tensor(dsl,
                                               S["twst"][:, (slot * E + e8) * 512:(slot * E + e8) * 512 + 512],
                                               S["gbc"][:, b * 8 + e8:b * 8 + e8 + 1],
                                               src1, ALU.mult, ALU.add))
                if b == 1:
                    vw = B.c["v"]
                    B.W("sync", "v", vw)
                    B.D("dms", lambda e, hc=hc, fb=fb, dramb1=dramb1:
                        e.dma_start(dr[dramb1][hc, :, fb * 512:(fb + 1) * 512], S["scrW"][:, 0:512]))
            vdone[si] = B.c["v"]
        f0_v = B.c["v"]
        marks["dms_total"] = B.c["dms"]

        if FSTAGE < 3:
            return
        mb1v_total = B.c["v"]
        B.W("scalar", "v", mb1v_total)
        for b in range(NB):
            if b == 1:
                B.W("sync", "t", marks[("f1tend", 0)])
                B.W("sync", "dms", marks["dms_total"])
                for hc in range(8):
                    B.D("dmw", lambda e, hc=hc: e.dma_start(S["mw1"][:, hc * F:(hc + 1) * F], dr["mw1b1"][hc]))
                for fcch in range(32):
                    B.D("dmw", lambda e, fcch=fcch: e.dma_start(S["mw2"][:, fcch * H:(fcch + 1) * H], dr["mw2b1"][fcch]))
                mwdx = B.c["dmw"]
                B.W("tensor", "dmw", mwdx)
            B.W("tensor", "v", mb2v if b == 0 else marks[("fbdone", 0)])
            for half in range(2):
                B.T(lambda t, b=b, half=half: t.matmul(
                    P[4][:, 0:512] if half == 0 else P[5][:, 0:512], S["onesrow"][:],
                    S["mb2row"][0:1, b * H + half * 512:b * H + half * 512 + 512], start=True, stop=True))
            tmb2 = B.c["t"]
            B.W("vector", "t", tmb2)
            B.V(lambda v: v.tensor_copy(S["mb2bc"][:, 0:512], P[4][:, 0:512]))
            mbbv = B.V(lambda v: v.tensor_copy(S["mb2bc"][:, 512:1024], P[5][:, 0:512]))

            for w in range(NW):
                wt0 = w * WIN
                B.W("sync", "t", marks.get(("f1t", b, w - 1), 0))
                for fcch in range(8):
                    B.D("dzr", lambda e, b=b, fcch=fcch, wt0=wt0:
                        e.dma_start(S["zrh"][:, fcch * WIN:(fcch + 1) * WIN],
                                    dr["zlns"][b, fcch, :, wt0:wt0 + WIN]))
                dzl = B.c["dzr"]
                B.W("tensor", "dzr", dzl)
                if b == 0 and w == 0:
                    B.W("tensor", "v", f0_v)
                B.W("scalar", "t", marks.get(("f2t", b, w - 1), 0))
                for fo in range(32):
                    pi = fo % 2
                    B.W("tensor", "s", marks.get(("gelu", b, w, fo - 2), 0))
                    for k in range(8):
                        B.T(lambda t, fo=fo, k=k, pi=pi: t.matmul(
                            P[pi][:, 0:WIN],
                            S["mw1"][:, k * F + fo * 128:k * F + fo * 128 + 128],
                            S["zrh"][:, k * WIN:(k + 1) * WIN],
                            start=(k == 0), stop=(k == 7)))
                    tf1 = B.c["t"]
                    B.W("scalar", "t", tf1)
                    sg = B.Sc(lambda s, fo=fo, pi=pi, b=b: s.activation(
                        S["u"][:, fo * WIN:(fo + 1) * WIN], P[pi][:, 0:WIN], AF.Gelu_apprx_tanh,
                        bias=S["mb1c"][:, fo * 2 + b:fo * 2 + b + 1]))
                    marks[("gelu", b, w, fo)] = sg
                marks[("f1t", b, w)] = B.c["t"]
                if DEBUG and b == 0 and w == 0:
                    B.W("sync", "s", marks[("gelu", 0, 0, 3)])
                    for fo in range(4):
                        B.D("ddb", lambda e, fo=fo: e.dma_start(dr["d_u"][fo], S["u"][:, fo * WIN:(fo + 1) * WIN]))
                for tt in range(WIN // 128):
                    dx1 = B.D("dxf", lambda e, b=b, wt0=wt0, tt=tt:
                              e.dma_start(S["xtf"][:], dr["x1_s"][b, wt0 + tt * 128:wt0 + tt * 128 + 128, :]))
                    B.W("tensor", "s", marks[("gelu", b, w, 31)])
                    B.W("tensor", "v", marks.get(("f2o", b, w, tt - 1), 0))
                    for ho in range(2):
                        pi = 2 + ho
                        for k in range(32):
                            B.T(lambda t, tt=tt, ho=ho, k=k, pi=pi: t.matmul(
                                P[pi][:, 0:512],
                                S["u"][:, k * WIN + tt * 128:k * WIN + tt * 128 + 128],
                                S["mw2"][:, k * H + ho * 512:k * H + ho * 512 + 512],
                                start=(k == 0), stop=(k == 31)))
                    tf2 = B.c["t"]
                    B.W("vector", "t", tf2)
                    B.W("vector", "dxf", dx1)
                    B.W("vector", "v", mbbv)
                    B.W("vector", "dou", B.c["dou"])
                    B.V(lambda v: v.tensor_tensor(S["scrF"][:, 0:512], P[2][:, 0:512], S["xtf"][:, 0:512], ALU.add))
                    B.V(lambda v: v.tensor_tensor(S["scrF"][:, 512:1024], P[3][:, 0:512],
                                                  S["xtf"][:, 512:1024], ALU.add))
                    fv = B.V(lambda v: v.tensor_tensor(S["scrF"][:], S["scrF"][:], S["mb2bc"][:], ALU.add))
                    B.W("sync", "v", fv)
                    B.D("dou", lambda e, b=b, wt0=wt0, tt=tt: e.dma_start(
                        dr["out"][b, wt0 + tt * 128:wt0 + tt * 128 + 128, :], S["scrF"][:]))
                    marks[("f2o", b, w, tt)] = B.c["v"]
                marks[("f2t", b, w)] = B.c["t"]
            marks[("f1tend", b)] = B.c["t"]
            marks[("fbdone", b)] = B.c["v"]
        for k in Bld.DKEYS:
            if k != "dg":
                B.W("sync", k, B.c[k])
        B.W("gpsimd", "dg", B.c["dg"])
    def plan_ffn_outer():
        plan_ffn()
        if FSTAGE < 99:
            for e in ("tensor", "vector", "scalar", "gpsimd", "sync"):
                B.W(e, "t", B.c["t"]); B.W(e, "v", B.c["v"]); B.W(e, "s", B.c["s"])
                for k in Bld.DKEYS:
                    B.W(e, k, B.c[k])

    def run_pass(eng, eb, which):
        B.eng, B.eb = eng, eb
        if which == "attn":
            B.reset()
            plan_attn()
            marks["snap_attn"] = dict(B.c)
        else:
            B.reset(marks["snap_attn"])
            if PHASES == "all":
                plan_ffn_outer()
            else:
                for e in ("tensor", "vector", "scalar", "gpsimd", "sync"):
                    ae = marks["attn_end"]
                    B.W(e, "t", ae["t"]); B.W(e, "v", ae["v"]); B.W(e, "s", ae["s"])
                    for k in Bld.DKEYS:
                        B.W(e, k, ae[k])

    with contextlib.ExitStack() as s0:
        for nm, sh, dt in (
            ("identf", (128, 128), F32), ("ltri", (128, 128), F32),
            ("tmask4", (128, 512), BF16), ("onescol", (128, 1), F32),
            ("onesrow", (1, 128), F32), ("econst", (64, 128), F32),
            ("msel", (64, 16), F32), ("hmask", (128, 64), F32),
            ("cos_t", (128, NCHUNK * 16), F32),
            ("sin_t", (128, NCHUNK * 16), F32), ("bbm_row", (1, 128), F32),
            ("bout_row", (1, H), F32), ("lngbc", (128, H), F32),
            ("lnbbc", (128, H), F32), ("gb1c", (128, 8), F32),
            ("gb2c", (8, 1), F32), ("b1c", (128, 32), F32),
            ("stat", (128, 16), F32), ("mbstat", (128, 16), F32),
            ("gacc", (8, NB * 128), F32), ("gcol", (8, NB), F32),
            ("grow", (1, 16), F32), ("gbc", (128, 8 * NB), F32),
            ("mb1c", (128, 64), F32), ("mb2row", (1, NB * H), F32),
            ("scrg", (8, 128), F32),
        ):
            B.sb(s0, nm, sh, dt)
        if DEBUG:
            B.sb(s0, "dbg", (128, 140), F32)
        for pi_, wdt in enumerate((1024, 1024, 512, 512, 512, 512)):
            P.append(s0.enter_context(nc.psum_tensor("ps%d" % pi_, [128, wdt], F32)))
        for n in ("t", "v", "s", "g") + Bld.DKEYS:
            B.sem[n] = s0.enter_context(nc.semaphore("sem_" + n))

        with contextlib.ExitStack() as sA:
            for nm, sh, dt in (
                ("wq", (128, 8 * H), BF16), ("wk", (128, 8 * H), BF16),
                ("wv", (128, 8 * H), BF16), ("wo", (128, 8 * H), BF16),
                ("wbm", (128, 8 * 128), BF16), ("gw1", (128, 8 * H), BF16),
                ("gw2", (128, 8 * 8), BF16),
                ("xt", (128, H), F32), ("xln", (128, H), F32),
                ("xlnfm", (128, 8 * 128), BF16),
                ("rotq", (128, H), F32), ("rotk", (128, H), F32),
                ("phiq", (128, H), F32), ("phik", (128, H), F32),
                ("scr1", (128, H), F32), ("scr2", (128, H), F32),
                ("phiqm", (128, 4 * 8 * 128), BF16), ("phikf", (128, 8 * 128), BF16),
                ("vstg", (128, NH * 33), BF16), ("vs", (128, NH * 66), BF16),
                ("ks", (128, M * H), BF16), ("bmt", (128, 128), F32),
                ("pip", (128, 64), F32), ("pp", (128, 64), F32),
                ("mixw", (128, 64), F32), ("plastc", (128, 32), F32),
                ("gsb", (128, 4 * 128), BF16),
                ("Tma", (128, NB * 16 * 132), F32), ("Sbf", (128, NB * 16 * 132), BF16),
                ("yt", (128, H), F32), ("yfm", (128, 8 * 128), BF16),
                ("zt", (128, H), F32), ("hg", (128, 8 * 128), BF16),
            ):
                B.sb(sA, nm, sh, dt)
            with nc.Block() as blkA:
                @blkA.tensor
                def _(t):
                    run_pass("tensor", t, "attn")

                @blkA.vector
                def _(v):
                    run_pass("vector", v, "attn")

                @blkA.scalar
                def _(s):
                    run_pass("scalar", s, "attn")

                @blkA.gpsimd
                def _(g):
                    run_pass("gpsimd", g, "attn")

                @blkA.sync
                def _(sy):
                    run_pass("sync", sy, "attn")

        with contextlib.ExitStack() as sB:
            for nm, sh, dt in (
                ("mw1", (128, 8 * F), BF16), ("mw2", (128, 32 * H), BF16),
                ("u", (128, 32 * WIN), BF16), ("zrh", (128, 8 * WIN), BF16),
                ("wst", (128, 1 * 512), BF16), ("twst", (128, 1 * E * 512), BF16),
                ("scrW", (128, 512), BF16), ("xtf", (128, H), F32),
                ("scrF", (128, H), F32), ("mb2bc", (128, H), F32),
                ("tb1st", (8, 128), F32), ("tb2s", (8, H), F32),
            ):
                B.sb(sB, nm, sh, dt)
            with nc.Block() as blkB:
                @blkB.tensor
                def _(t):
                    run_pass("tensor", t, "ffn")

                @blkB.vector
                def _(v):
                    run_pass("vector", v, "ffn")

                @blkB.scalar
                def _(s):
                    run_pass("scalar", s, "ffn")

                @blkB.gpsimd
                def _(g):
                    run_pass("gpsimd", g, "ffn")

                @blkB.sync
                def _(sy):
                    run_pass("sync", sy, "ffn")

    nc.finalize()
    return nc


_NC_CACHE = None


def kernel(**inputs):
    global _NC_CACHE
    weights = make_weight_map(inputs)
    if _NC_CACHE is None:
        _NC_CACHE = build_program()
    x = np.asarray(inputs["L0_attn_x"])
    in_maps = []
    for c in range(8):
        m = dict(weights)
        m["x"] = np.ascontiguousarray(x[2 * c:2 * c + 2])
        in_maps.append(m)
    res = run_bass_kernel_spmd(_NC_CACHE, in_maps, core_ids=list(range(8)))
    return np.concatenate([r["out"] for r in res.results], axis=0)


# version 05 (final)
# speedup vs baseline: 1.2137x; 1.2137x over previous
import contextlib
import numpy as np
import ml_dtypes
import concourse.bass as bass
import concourse.mybir as mybir
from concourse.bass_utils import run_bass_kernel_spmd

F32 = mybir.dt.float32
BF16 = mybir.dt.bfloat16
AF = mybir.ActivationFunctionType
ALU = mybir.AluOpType
BF = ml_dtypes.bfloat16
AX = mybir.AxisListType.X

NB, T, C, H, NH, D, M, F, E = 2, 2048, 128, 1024, 32, 32, 2, 4096, 8
NCHUNK = T // C
LN_EPS = 1e-5
DEN_EPS = 1e-6
BMIN, BMAX = 0.85, 0.9995
WIN = 256
NW = T // WIN

DEBUG = False
PHASES = "all"
STAGE = 99
NCI = 99
SUB = 99
FSTAGE = 99
FSUB = 99


def host_consts():
    c = {}
    c["identf"] = np.eye(128, dtype=np.float32)
    jj, ii = np.meshgrid(np.arange(128), np.arange(128), indexing="ij")
    c["ltri"] = (jj <= ii).astype(np.float32)
    c["tmask4"] = np.tile((jj <= ii).astype(BF), (1, 4))
    c["onescol"] = np.ones((128, 1), np.float32)
    hm = np.zeros((128, 64), np.float32)
    for p in range(128):
        hm[p, p // 32] = 1.0
    c["hmask"] = hm
    c["onesrow"] = np.ones((1, 128), np.float32)
    ee = np.zeros((64, 128), np.float32)
    for r in range(64):
        ee[r, ((r // 2) % 4) * 32:((r // 2) % 4) * 32 + 32] = 1.0
    c["econst"] = ee
    ms = np.zeros((64, 16), np.float32)
    for r in range(64):
        ms[r, 2 * ((r // 2) // 4) + (r % 2)] = 1.0
    c["msel"] = ms
    half = D // 2
    inv_freq = (1.0 / (10000.0 ** (np.arange(half, dtype=np.float32) / np.float32(half)))).astype(np.float32)
    t = np.arange(T, dtype=np.float32)
    freqs = (t[:, None] * inv_freq[None, :]).astype(np.float32)
    cos = np.cos(freqs).astype(np.float32)
    sin = np.sin(freqs).astype(np.float32)
    c["cos_t"] = cos.reshape(NCHUNK, 128, half).transpose(1, 0, 2).reshape(128, NCHUNK * half).copy()
    c["sin_t"] = sin.reshape(NCHUNK, 128, half).transpose(1, 0, 2).reshape(128, NCHUNK * half).copy()
    return c


_CONSTS = host_consts()


def make_weight_map(inputs):
    g = lambda k: np.asarray(inputs[k])
    w = {}
    w["wqT"] = np.ascontiguousarray(g("L0_attn_Wq").T)
    w["wkT"] = np.ascontiguousarray(g("L0_attn_Wk").T)
    w["wvT"] = np.ascontiguousarray(g("L0_attn_Wv").T)
    w["woutT"] = np.ascontiguousarray(g("L0_attn_Wout").T)
    w["wbmT"] = np.ascontiguousarray(np.concatenate([g("L0_attn_Wbeta"), g("L0_attn_Wmix")], 0).T)
    w["bbm_row"] = np.ascontiguousarray(np.concatenate([g("L0_attn_bbeta"), g("L0_attn_bmix")])[None, :])
    w["bout_row"] = np.ascontiguousarray(g("L0_attn_bout")[None, :])
    w["lng_row"] = np.ascontiguousarray(g("L0_attn_ln_g")[None, :])
    w["lnb_row"] = np.ascontiguousarray(g("L0_attn_ln_b")[None, :])
    w["gw1T"] = np.ascontiguousarray(g("L0_ffn_gate_w1").T)
    w["gb1c"] = np.ascontiguousarray(g("L0_ffn_gate_b1").reshape(8, 128).T)
    w["gw2T"] = np.ascontiguousarray(g("L0_ffn_gate_w2").T)
    w["gb2c"] = np.ascontiguousarray(g("L0_ffn_gate_b2").reshape(8, 1))
    w["w1T"] = np.ascontiguousarray(g("L0_ffn_W1").T)
    w["tw1T"] = np.ascontiguousarray(g("L0_ffn_TW1").transpose(0, 2, 1))
    w["b1c"] = np.ascontiguousarray(g("L0_ffn_b1").reshape(32, 128).T)
    w["tb1"] = np.ascontiguousarray(g("L0_ffn_Tb1"))
    w["w2T"] = np.ascontiguousarray(g("L0_ffn_W2").T)
    w["tw2T"] = np.ascontiguousarray(g("L0_ffn_TW2").transpose(0, 2, 1))
    w["b2row"] = np.ascontiguousarray(g("L0_ffn_b2")[None, :])
    w["tb2"] = np.ascontiguousarray(g("L0_ffn_Tb2"))
    w.update(_CONSTS)
    return w


class Bld:
    def __init__(self, nc):
        self.nc = nc
        self.dr = {}
        self.S = {}
        self.P = []
        self.sem = {}
        self.eng = None
        self.eb = None
        self.marks = {}

    def din(self, name, shape, dt=F32):
        self.dr[name] = self.nc.dram_tensor(name, list(shape), dt, kind="ExternalInput").ap()

    def dout(self, name, shape, dt=F32):
        self.dr[name] = self.nc.dram_tensor(name, list(shape), dt, kind="ExternalOutput").ap()

    def dint(self, name, shape, dt=F32):
        self.dr[name] = self.nc.dram_tensor(name, list(shape), dt, kind="Internal").ap()

    def sb(self, stack, name, shape, dt=F32):
        self.S[name] = stack.enter_context(self.nc.sbuf_tensor("s_" + name, list(shape), dt))

    DKEYS = ("dc", "dxi", "dzo", "dxo", "dou", "dzr", "dxf", "dmw", "dms", "dtb", "ddb", "dg")

    def reset(self, snap=None):
        base = dict(t=0, v=0, s=0, g=0)
        base.update({k: 0 for k in self.DKEYS})
        self.c = dict(snap) if snap else base

    SELFWAIT = ("v", "s", "g")

    def _op(self, key, which, fn, n=1):
        if key in self.SELFWAIT and self.eng == which and self.c[key] > 0:
            self.eb.wait_ge(self.sem[key], self.c[key])
        self.c[key] += n
        if self.eng == which:
            fn(self.eb).then_inc(self.sem[key], n)
        return self.c[key]

    def T(self, fn):
        return self._op("t", "tensor", fn)

    def V(self, fn):
        return self._op("v", "vector", fn)

    def Sc(self, fn):
        return self._op("s", "scalar", fn)

    def D(self, key, fn):
        return self._op(key, "sync", fn, 16)

    def DG(self, fn):
        return self._op("dg", "gpsimd", fn, 16)

    def W(self, eng, key, val):
        if self.eng == eng and val > 0:
            self.eb.wait_ge(self.sem[key], val)

    def ap(self, name, dims, off=0):
        a = self.S[name][:]
        pitch = a.ap[0][0]
        dd = [list(d) for d in dims]
        dd[0] = [pitch, dd[0][1]]
        return bass.AP(a.tensor, off, dd)

    def app(self, ps_idx, dims, off=0):
        a = self.P[ps_idx][:]
        pitch = a.ap[0][0]
        dd = [list(d) for d in dims]
        dd[0] = [pitch, dd[0][1]]
        return bass.AP(a.tensor, off, dd)


def build_program():
    nc = bass.Bass("TRN2", target_bir_lowering=False, detect_race_conditions=False)
    B = Bld(nc)
    S, P, dr, marks = B.S, B.P, B.dr, B.marks

    B.din("x", (NB, T, H))
    for n in ("wqT", "wkT", "wvT", "woutT"):
        B.din(n, (H, H))
    B.din("wbmT", (H, 128)); B.din("bbm_row", (1, 128))
    B.din("bout_row", (1, H)); B.din("lng_row", (1, H)); B.din("lnb_row", (1, H))
    B.din("gw1T", (H, H)); B.din("gb1c", (128, 8)); B.din("gw2T", (H, 8)); B.din("gb2c", (8, 1))
    B.din("w1T", (H, F)); B.din("tw1T", (E, H, F)); B.din("b1c", (128, 32)); B.din("tb1", (E, F))
    B.din("w2T", (F, H)); B.din("tw2T", (E, F, H)); B.din("b2row", (1, H)); B.din("tb2", (E, H))
    B.din("identf", (128, 128)); B.din("ltri", (128, 128)); B.din("tmask4", (128, 512), BF16)
    B.din("onescol", (128, 1)); B.din("onesrow", (1, 128)); B.din("hmask", (128, 64))
    B.din("econst", (64, 128)); B.din("msel", (64, 16))
    B.din("cos_t", (128, NCHUNK * 16)); B.din("sin_t", (128, NCHUNK * 16))

    B.dout("out", (NB, T, H))
    B.dint("x1_s", (NB, T, H))
    B.dint("zlns", (NB, 8, 128, T), BF16)
    B.dint("mw1b1", (8, 128, F), BF16)
    B.dint("mw2b1", (32, 128, H), BF16)
    if DEBUG:
        for n, sh in (("d_xln", (NB * NCHUNK, 128, H)), ("d_phiq", (NB * NCHUNK, 128, H)),
                      ("d_phik", (NB * NCHUNK, 128, H)),
                      ("d_pip", (NB * NCHUNK, 128, 128)), ("d_y", (NB * NCHUNK, 128, H)),
                      ("d_x1", (NB * NCHUNK, 128, H)), ("d_g", (8, NB)),
                      ("d_mb1", (128, 64))):
            B.dout(n, sh)
        B.dout("d_v", (NB * NCHUNK, 128, NH * 33), BF16)
        B.dout("d_u", (4, 128, WIN), BF16)
        B.dout("d_hg", (2, 128, 8 * 128), BF16)
        B.dout("d_chain", (6, 128, 64))
        B.dout("d_s8", (6, 128, 140))
        B.dout("d_gacc", (NB * NCHUNK, 8, 128))

    def plan_attn():
        for n in ("identf", "ltri", "onescol", "onesrow", "econst", "msel",
                  "cos_t", "sin_t", "bbm_row", "bout_row", "gb1c", "gb2c", "b1c",
                  "tmask4", "hmask"):
            B.D("dc", lambda e, n=n: e.dma_start(S[n][:], dr[n]))
        B.D("dc", lambda e: e.dma_start(S["lngbc"][0:1, :], dr["lng_row"]))
        cdx = B.D("dc", lambda e: e.dma_start(S["lnbbc"][0:1, :], dr["lnb_row"]))
        B.W("tensor", "dc", cdx)
        for half in range(2):
            B.T(lambda t, half=half: t.matmul(P[0][:, half * 512:(half + 1) * 512], S["onesrow"][:],
                                              S["lngbc"][0:1, half * 512:(half + 1) * 512],
                                              start=True, stop=True))
            B.T(lambda t, half=half: t.matmul(P[1][:, half * 512:(half + 1) * 512], S["onesrow"][:],
                                              S["lnbbc"][0:1, half * 512:(half + 1) * 512],
                                              start=True, stop=True))
        tb = B.c["t"]
        B.W("vector", "t", tb)
        B.V(lambda v: v.tensor_copy(S["lngbc"][:], P[0][:, 0:1024]))
        lnv = B.V(lambda v: v.tensor_copy(S["lnbbc"][:], P[1][:, 0:1024]))
        marks["lnv"] = lnv
        for (wname, dname, width) in (("wq", "wqT", H), ("wk", "wkT", H), ("wv", "wvT", H),
                                      ("wo", "woutT", H), ("wbm", "wbmT", 128),
                                      ("gw1", "gw1T", H), ("gw2", "gw2T", 8)):
            for k in range(8):
                B.DG(lambda e, wname=wname, dname=dname, width=width, k=k:
                     e.dma_start(S[wname][:, k * width:(k + 1) * width],
                                 dr[dname][k * 128:(k + 1) * 128, :]))
        marks["wdg"] = B.c["dg"]
        marks["cdx"] = cdx
        B.W("vector", "dc", cdx)
        if DEBUG:
            dv0 = B.V(lambda v: v.tensor_copy(S["dbg"][:, 0:4], S["hmask"][:, 0:4]))
            B.W("sync", "v", dv0)
            B.D("ddb", lambda e: e.dma_start(dr["d_s8"][3][:, 70:74], S["dbg"][:, 0:4]))
        B.V(lambda v: v.memset(S["Tma"][:], 0.0))
        B.V(lambda v: v.memset(S["Sbf"][:], 0.0))
        B.V(lambda v: v.memset(S["vstg"][:], 0.0))
        B.V(lambda v: v.memset(B.ap("vstg", [[1, 128], [33, NH], [1, 1]], off=32), 1.0))
        B.V(lambda v: v.memset(S["plastc"][:], 1.0))
        B.V(lambda v: v.memset(S["gacc"][:], 0.0))

        nci_done = 0
        for b in range(NB):
            for cc in range(NCHUNK):
                if nci_done < NCI:
                    attn_ci(b, cc)
                    nci_done += 1
        marks["attn_end"] = dict(B.c)

    def ln_stats_apply(src, dst):
        B.Sc(lambda s: s.activation(S["scr1"][:], S[src][:], AF.Copy, accum_out=S["stat"][:, 0:1]))
        s2 = B.Sc(lambda s: s.activation(S["scr2"][:], S[src][:], AF.Square, accum_out=S["stat"][:, 1:2]))
        B.W("vector", "s", s2)
        B.V(lambda v: v.tensor_scalar(S["stat"][:, 2:3], S["stat"][:, 0:1], 1.0 / H, None, ALU.mult))
        B.V(lambda v: v.tensor_tensor(S["stat"][:, 3:4], S["stat"][:, 2:3], S["stat"][:, 2:3], ALU.mult))
        vv = B.V(lambda v: v.tensor_scalar(S["stat"][:, 4:5], S["stat"][:, 1:2], 1.0 / H, S["stat"][:, 3:4],
                                           ALU.mult, ALU.subtract))
        vv = B.V(lambda v: v.tensor_scalar(S["stat"][:, 4:5], S["stat"][:, 4:5], LN_EPS, None, ALU.add))
        B.W("scalar", "v", vv)
        B.Sc(lambda s: s.activation(S["stat"][:, 5:6], S["stat"][:, 4:5], AF.Ln))
        sr = B.Sc(lambda s: s.activation(S["stat"][:, 6:7], S["stat"][:, 5:6], AF.Exp, scale=-0.5))
        B.W("vector", "s", sr)
        B.V(lambda v: v.tensor_tensor(S["stat"][:, 7:8], S["stat"][:, 2:3], S["stat"][:, 6:7], ALU.mult))
        return B.V(lambda v: v.tensor_scalar(S[dst][:], S[src][:], S["stat"][:, 6:7], S["stat"][:, 7:8],
                                             ALU.mult, ALU.subtract))

    def fm_transpose(src, dst):
        for fc in range(8):
            pi, poff = (4, (fc % 4) * 128) if fc < 4 else (5, (fc % 4) * 128)
            B.T(lambda t, fc=fc, pi=pi, poff=poff:
                t.transpose(P[pi][:, poff:poff + 128], S[src][:, fc * 128:(fc + 1) * 128], S["identf"][:]))
        tt = B.c["t"]
        B.W("vector", "t", tt)
        B.V(lambda v: v.tensor_copy(S[dst][:, 0:512], P[4][:, 0:512]))
        return B.V(lambda v: v.tensor_copy(S[dst][:, 512:1024], P[5][:, 0:512]))

    def attn_ci(b, cc):
        ci = b * NCHUNK + cc
        t0 = cc * C

        B.W("sync", "v", marks.get(("cidone", ci - 1), 0))
        B.W("sync", "dxo", B.c["dxo"])
        dxv = B.D("dxi", lambda e: e.dma_start(S["xt"][:], dr["x"][b, t0:t0 + C, :]))
        B.W("scalar", "dxi", dxv)
        if ci == 0:
            B.W("scalar", "v", marks["lnv"])
        xlnv = ln_stats_apply("xt", "xln")
        if DEBUG:
            B.W("sync", "v", xlnv)
            B.D("ddb", lambda e: e.dma_start(dr["d_xln"][ci], S["xln"][:]))
        if STAGE < 2:
            marks[("cidone", ci)] = B.c["v"]
            return
        B.W("tensor", "v", xlnv)
        B.W("vector", "dzo", B.c["dzo"])
        exv = fm_transpose("xln", "xlnfm")
        if STAGE < 3:
            marks[("cidone", ci)] = B.c["v"]
            return
        B.W("tensor", "v", exv)
        if ci == 0:
            B.W("tensor", "dg", marks["wdg"])
        def proj(w, oc0, width, pi, poff, wwidth=H, close=True):
            for k in range(8):
                B.T(lambda t, w=w, oc0=oc0, width=width, pi=pi, poff=poff, k=k, wwidth=wwidth, close=close:
                    t.matmul(P[pi][:, poff:poff + width],
                             S["xlnfm"][:, k * 128:(k + 1) * 128],
                             S[w][:, k * wwidth + oc0:k * wwidth + oc0 + width],
                             start=(k == 0), stop=(k == 7 and close)))
        proj("wq", 0, 512, 0, 0)
        proj("wq", 512, 512, 0, 512)
        proj("wk", 0, 512, 1, 0)
        proj("wk", 512, 512, 1, 512)
        proj("wv", 0, 512, 2, 0)
        proj("wv", 512, 512, 3, 0)
        proj("wbm", 0, 128, 4, 0, wwidth=128, close=False)
        tproj = B.T(lambda t: t.matmul(P[4][:, 0:128], S["onesrow"][:], S["bbm_row"][:],
                                       start=False, stop=True))

        if STAGE < 4:
            marks[("cidone", ci)] = B.c["v"]
            return
        B.W("vector", "t", tproj)
        co = cc * 16
        for (src_pi, dst) in ((0, "rotq"), (1, "rotk")):
            u1 = B.app(src_pi, [[1, 128], [32, 32], [1, 16]], off=0)
            u2 = B.app(src_pi, [[1, 128], [32, 32], [1, 16]], off=16)
            cosb = B.ap("cos_t", [[1, 128], [0, 32], [1, 16]], off=co)
            sinb = B.ap("sin_t", [[1, 128], [0, 32], [1, 16]], off=co)
            o1 = B.ap(dst, [[1, 128], [32, 32], [1, 16]], off=0)
            o2 = B.ap(dst, [[1, 128], [32, 32], [1, 16]], off=16)
            B.V(lambda v, u2=u2, sinb=sinb: v.tensor_tensor(S["scr1"][:, 0:512], u2, sinb, ALU.mult))
            B.V(lambda v, u1=u1, cosb=cosb: v.tensor_tensor(S["scr2"][:, 0:512], u1, cosb, ALU.mult))
            B.V(lambda v, o1=o1: v.tensor_tensor(o1, S["scr2"][:, 0:512], S["scr1"][:, 0:512], ALU.subtract))
            B.V(lambda v, u1=u1, sinb=sinb: v.tensor_tensor(S["scr1"][:, 0:512], u1, sinb, ALU.mult))
            B.V(lambda v, u2=u2, cosb=cosb: v.tensor_tensor(S["scr2"][:, 0:512], u2, cosb, ALU.mult))
            B.V(lambda v, o2=o2: v.tensor_tensor(o2, S["scr2"][:, 0:512], S["scr1"][:, 0:512], ALU.add))
        vd1 = B.ap("vstg", [[1, 128], [33, 16], [1, 32]], off=0)
        vd2 = B.ap("vstg", [[1, 128], [33, 16], [1, 32]], off=16 * 33)
        B.V(lambda v, vd1=vd1: v.tensor_copy(vd1, B.app(2, [[1, 128], [32, 16], [1, 32]], off=0)))
        ve1 = B.V(lambda v, vd2=vd2: v.tensor_copy(vd2, B.app(3, [[1, 128], [32, 16], [1, 32]], off=0)))
        if DEBUG:
            B.W("sync", "v", ve1)
            B.D("ddb", lambda e: e.dma_start(dr["d_v"][ci], S["vstg"][:]))
        B.W("scalar", "t", tproj)
        se = B.Sc(lambda s: s.activation(S["bmt"][:, 0:64], P[4][:, 0:64], AF.Exp, scale=-1.0))
        B.W("vector", "s", se)
        B.V(lambda v: v.tensor_scalar(S["bmt"][:, 0:64], S["bmt"][:, 0:64], 1.0, None, ALU.add))
        B.V(lambda v: v.reciprocal(S["scr1"][:, 0:64], S["bmt"][:, 0:64]))
        bclip = B.V(lambda v: v.tensor_scalar(S["bmt"][:, 0:64], S["scr1"][:, 0:64], BMIN, BMAX,
                                              ALU.max, ALU.min))
        B.W("scalar", "v", bclip)
        slog = B.Sc(lambda s: s.activation(S["bmt"][:, 64:128], S["bmt"][:, 0:64], AF.Ln))
        vmix = B.V(lambda v: v.tensor_copy(S["scr2"][:, 0:64], P[4][:, 64:128]))
        a1 = B.ap("scr2", [[1, 128], [2, 32], [1, 1]], off=0)
        a2 = B.ap("scr2", [[1, 128], [2, 32], [1, 1]], off=1)
        B.V(lambda v, a1=a1, a2=a2: v.tensor_tensor(S["scr1"][:, 0:32], a1, a2, ALU.subtract))
        sm = B.Sc(lambda s: s.activation(S["scr1"][:, 32:64], S["scr1"][:, 0:32], AF.Exp, scale=-1.0))
        B.W("vector", "s", sm)
        B.V(lambda v: v.tensor_scalar(S["scr1"][:, 32:64], S["scr1"][:, 32:64], 1.0, None, ALU.add))
        B.V(lambda v: v.reciprocal(S["scr2"][:, 0:32], S["scr1"][:, 32:64]))
        mx1 = B.ap("mixw", [[1, 128], [2, 32], [1, 1]], off=0)
        mx2 = B.ap("mixw", [[1, 128], [2, 32], [1, 1]], off=1)
        B.V(lambda v, mx1=mx1: v.tensor_copy(mx1, S["scr2"][:, 0:32]))
        mixv = B.V(lambda v, mx2=mx2: v.tensor_scalar(mx2, S["scr2"][:, 0:32], -1.0, 1.0, ALU.mult, ALU.add))

        if STAGE < 5:
            marks[("cidone", ci)] = B.c["v"]
            return
        B.W("tensor", "s", slog)
        B.W("tensor", "v", vmix)
        B.T(lambda t: t.matmul(P[4][:, 128:192], S["ltri"][:], S["bmt"][:, 64:128], start=True, stop=True))
        tcl = B.T(lambda t: t.matmul(P[4][0:64, 192:193], S["bmt"][:, 64:128], S["onescol"][:],
                                     start=True, stop=True))
        B.W("scalar", "t", tcl)
        B.Sc(lambda s: s.activation(S["pp"][:], P[4][:, 128:192], AF.Exp))
        sip = B.Sc(lambda s: s.activation(S["pip"][:], P[4][:, 128:192], AF.Exp, scale=-1.0))
        if DEBUG:
            B.W("vector", "s", sip)
            dv1 = B.V(lambda v: v.tensor_copy(S["scr1"][:, 0:64], S["pp"][:]))
            dv2 = B.V(lambda v: v.tensor_copy(S["scr1"][:, 64:128], S["pip"][:]))
            B.W("sync", "v", dv2)
            B.D("ddb", lambda e: e.dma_start(dr["d_pip"][ci], S["scr1"][:, 0:128]))
        B.W("vector", "t", tcl)
        clap = B.app(4, [[1, 64], [0, 16]], off=192)
        mselv = B.V(lambda v, clap=clap: v.tensor_tensor(S["mbstat"][0:64, 0:16], S["msel"][:], clap, ALU.mult))
        B.W("tensor", "v", mselv)
        B.W("tensor", "s", sip)
        tpl = B.T(lambda t: t.matmul(P[4][:, 208:224], S["econst"][:], S["mbstat"][0:64, 0:16],
                                     start=True, stop=True))
        B.W("scalar", "t", tpl)
        slot = (ci % 2) * 16
        pslot = B.Sc(lambda s, slot=slot: s.activation(S["plastc"][:, slot:slot + 16], P[4][:, 208:224], AF.Exp))
        if DEBUG and ci == 0:
            B.W("vector", "s", pslot)
            srcs = [("bmt", 0, 64), ("bmt", 64, 64), (4, 128, 64), ("pp", 0, 64),
                    ("pip", 0, 64), ("plastc", slot, 16)]
            for i6, (nm, off, wd) in enumerate(srcs):
                B.W("vector", "ddb", B.c["ddb"])
                src = P[4][:, off:off + wd] if nm == 4 else S[nm][:, off:off + wd]
                dv_ = B.V(lambda v, src=src, wd=wd: v.tensor_copy(S["dbg"][:, 0:wd], src))
                B.W("sync", "v", dv_)
                B.D("ddb", lambda e, i6=i6, wd=wd: e.dma_start(dr["d_chain"][i6][:, 0:wd], S["dbg"][:, 0:wd]))

        if STAGE < 6:
            marks[("cidone", ci)] = B.c["v"]
            return
        vminq = B.V(lambda v: v.tensor_scalar_min(S["scr1"][:], S["rotq"][:], 0.0))
        B.W("scalar", "v", vminq)
        B.Sc(lambda s: s.activation(S["scr2"][:], S["scr1"][:], AF.Exp))
        sphi = B.Sc(lambda s: s.activation(S["phiq"][:], S["rotq"][:], AF.Relu))
        B.W("vector", "s", sphi)
        B.V(lambda v: v.tensor_tensor(S["phiq"][:], S["phiq"][:], S["scr2"][:], ALU.add))
        vmink = B.V(lambda v: v.tensor_scalar_min(S["scr1"][:], S["rotk"][:], 0.0))
        B.W("scalar", "v", vmink)
        B.Sc(lambda s: s.activation(S["scr2"][:], S["scr1"][:], AF.Exp))
        sphk = B.Sc(lambda s: s.activation(S["phik"][:], S["rotk"][:], AF.Relu))
        B.W("vector", "s", sphk)
        phkv = B.V(lambda v: v.tensor_tensor(S["phik"][:], S["phik"][:], S["scr2"][:], ALU.add))
        if DEBUG:
            B.W("sync", "v", phkv)
            B.D("ddb", lambda e: e.dma_start(dr["d_phiq"][ci], S["phiq"][:]))
            B.D("ddb", lambda e: e.dma_start(dr["d_phik"][ci], S["phik"][:]))
        B.W("tensor", "v", phkv)
        for fc in range(8):
            B.T(lambda t, fc=fc: t.transpose(P[0][:, fc * 128:(fc + 1) * 128],
                                             S["phiq"][:, fc * 128:(fc + 1) * 128], S["identf"][:]))
        for fc in range(8):
            B.T(lambda t, fc=fc: t.transpose(P[1][:, fc * 128:(fc + 1) * 128],
                                             S["phik"][:, fc * 128:(fc + 1) * 128], S["identf"][:]))
        tphT = B.c["t"]
        B.W("vector", "t", tphT)
        B.W("scalar", "t", tphT)
        for hl in range(4):
            for hf in range(2):
                mb_ap = B.ap("hmask", [[1, 128], [0, 512]], off=hl)
                B.V(lambda v, hl=hl, hf=hf, mb_ap=mb_ap: v.tensor_tensor(
                    S["phiqm"][:, hl * 1024 + hf * 512:hl * 1024 + hf * 512 + 512],
                    P[0][:, hf * 512:hf * 512 + 512], mb_ap, ALU.mult))
        phfv = B.V(lambda v: v.tensor_copy(S["phikf"][:], P[1][:, 0:1024]))

        if STAGE < 7:
            marks[("cidone", ci)] = B.c["v"]
            return
        B.W("vector", "s", sip)
        for m in range(M):
            ipb = B.ap("pip", [[1, 128], [2, 32], [0, 32]], off=m)
            B.V(lambda v, m=m, ipb=ipb: v.tensor_tensor(S["ks"][:, m * H:(m + 1) * H], S["phik"][:], ipb, ALU.mult))
            vsb = B.ap("vs", [[1, 128], [66, 32], [1, 32]], off=m * 33)
            vsr = B.ap("vstg", [[1, 128], [33, 32], [1, 32]], off=0)
            B.V(lambda v, vsb=vsb, vsr=vsr, ipb=ipb: v.tensor_tensor(vsb, vsr, ipb, ALU.mult))
            ipd = B.ap("vs", [[1, 128], [66, 32], [1, 1]], off=m * 33 + 32)
            ips = B.ap("pip", [[1, 128], [2, 32], [1, 1]], off=m)
            B.V(lambda v, ipd=ipd, ips=ips: v.tensor_copy(ipd, ips))
        opsv = B.c["v"]

        if STAGE < 8:
            marks[("cidone", ci)] = B.c["v"]
            return
        B.W("tensor", "v", opsv)
        B.W("tensor", "s", pslot)
        prevmarks = {}
        for g8 in range(8):
            if g8 > 0:
                B.W("tensor", "v", prevmarks["gev"])
            for hl in range(4):
                B.T(lambda t, g8=g8, hl=hl: t.matmul(
                    P[4][:, hl * 128:(hl + 1) * 128],
                    S["phikf"][:, g8 * 128:(g8 + 1) * 128],
                    S["phiqm"][:, hl * 1024 + g8 * 128:hl * 1024 + (g8 + 1) * 128],
                    start=True, stop=True))
            tg = B.c["t"]
            B.W("vector", "t", tg)
            gev = B.V(lambda v: v.tensor_tensor(S["gsb"][:], P[4][:, 0:512], S["tmask4"][:], ALU.mult))
            prevmarks["gev"] = gev
            B.W("tensor", "v", gev)
            if g8 > 0:
                B.W("tensor", "v", prevmarks["cmb"])
            for hl in range(4):
                h = 4 * g8 + hl
                if SUB < 3:
                    continue
                B.T(lambda t, hl=hl, h=h: t.matmul(
                    P[5][:, hl * 66:hl * 66 + 66],
                    S["gsb"][:, hl * 128:(hl + 1) * 128],
                    S["vs"][:, h * 66:(h + 1) * 66],
                    start=True, stop=(SUB < 4)))
                if SUB < 4:
                    continue
                for m in range(M):
                    soff = (b * 16 + g8 * 2 + m) * 132
                    B.T(lambda t, hl=hl, m=m, soff=soff, g8=g8: t.matmul(
                        P[5][:, hl * 66 + m * 33:hl * 66 + m * 33 + 33],
                        S["phiqm"][:, hl * 1024 + g8 * 128:hl * 1024 + (g8 + 1) * 128],
                        S["Sbf"][:, soff + 33 * hl:soff + 33 * hl + 33],
                        start=False, stop=(m == M - 1)))
            tav = B.c["t"]
            if g8 > 0:
                B.W("tensor", "v", prevmarks["stv"])
            for m in range(M):
                if SUB < 5:
                    continue
                B.T(lambda t, g8=g8, m=m: t.matmul(
                    P[2][:, m * 132:(m + 1) * 132],
                    S["ks"][:, m * H + g8 * 128:m * H + (g8 + 1) * 128],
                    S["vstg"][:, 4 * g8 * 33:4 * g8 * 33 + 132],
                    start=True, stop=True))
            tst = B.c["t"]
            if SUB < 6:
                continue

            B.W("vector", "t", tav)
            B.W("vector", "s", pslot)
            dra = B.app(5, [[1, 128], [66, 4], [33, 2]], off=32)
            ppg = B.ap("pp", [[1, 128], [2, 4], [1, 2]], off=8 * g8)
            B.V(lambda v, dra=dra, ppg=ppg: v.tensor_tensor(S["mbstat"][:, 0:8], dra, ppg, ALU.mult))
            B.V(lambda v: v.tensor_scalar(S["mbstat"][:, 0:8], S["mbstat"][:, 0:8], DEN_EPS, None, ALU.add))
            B.V(lambda v: v.reciprocal(S["mbstat"][:, 8:16], S["mbstat"][:, 0:8]))
            mwg = B.ap("mixw", [[1, 128], [2, 4], [1, 2]], off=8 * g8)
            B.V(lambda v, mwg=mwg, ppg=ppg: v.tensor_tensor(S["mbstat"][:, 0:8], mwg, ppg, ALU.mult))
            B.V(lambda v: v.tensor_tensor(S["mbstat"][:, 0:8], S["mbstat"][:, 0:8], S["mbstat"][:, 8:16], ALU.mult))
            for m in range(M):
                numm = B.app(5, [[1, 128], [66, 4], [1, 32]], off=m * 33)
                wmb = B.ap("mbstat", [[1, 128], [2, 4], [0, 32]], off=m)
                dst = (S["scr1"] if m == 0 else S["scr2"])[:, 0:128]
                B.V(lambda v, numm=numm, wmb=wmb, dst=dst: v.tensor_tensor(dst, numm, wmb, ALU.mult))
            B.V(lambda v, g8=g8: v.tensor_tensor(S["yt"][:, g8 * 128:(g8 + 1) * 128],
                                                 S["scr1"][:, 0:128], S["scr2"][:, 0:128], ALU.add))
            prevmarks["cmb"] = B.c["v"]
            B.W("vector", "t", tst)
            pprev = ((ci + 1) % 2) * 16
            for m in range(M):
                soff = (b * 16 + g8 * 2 + m) * 132
                pcp = S["plastc"][:, pprev + 2 * g8 + m:pprev + 2 * g8 + m + 1]
                pcc = S["plastc"][:, slot + 2 * g8 + m:slot + 2 * g8 + m + 1]
                B.V(lambda v, soff=soff, pcp=pcp, m=m: v.scalar_tensor_tensor(
                    S["Tma"][:, soff:soff + 132], S["Tma"][:, soff:soff + 132], pcp,
                    P[2][:, m * 132:(m + 1) * 132], ALU.mult, ALU.add))
                B.V(lambda v, soff=soff, pcc=pcc: v.tensor_scalar(
                    S["Sbf"][:, soff:soff + 132], S["Tma"][:, soff:soff + 132], pcc, None, ALU.mult))
            stv = B.c["v"]
            prevmarks["stv"] = stv
        yv_all = B.c["v"]
        if DEBUG and ci == 0:
            srcs2 = [("phiqm", 0, 128), ("phiqm", 1024, 128), ("gsb", 0, 128),
                     ("vs", 0, 66), ("Sbf", (b * 16) * 132, 132), ("ks", 0, 128)]
            for i6, (nm, off, wd) in enumerate(srcs2):
                B.W("vector", "ddb", B.c["ddb"])
                dv_ = B.V(lambda v, nm=nm, off=off, wd=wd: v.tensor_copy(S["dbg"][:, 0:wd], S[nm][:, off:off + wd]))
                B.W("sync", "v", dv_)
                B.D("ddb", lambda e, i6=i6, wd=wd: e.dma_start(dr["d_s8"][i6][:, 0:wd], S["dbg"][:, 0:wd]))
        if DEBUG:
            B.W("sync", "v", yv_all)
            B.D("ddb", lambda e: e.dma_start(dr["d_y"][ci], S["yt"][:]))

        if STAGE < 9:
            marks[("cidone", ci)] = B.c["v"]
            return
        B.W("tensor", "v", yv_all)
        for fc in range(8):
            B.T(lambda t, fc=fc: t.transpose(P[0][:, fc * 128:(fc + 1) * 128],
                                             S["yt"][:, fc * 128:(fc + 1) * 128], S["identf"][:]))
        tyT = B.c["t"]
        B.W("vector", "t", tyT)
        yfv = B.V(lambda v: v.tensor_copy(S["yfm"][:], P[0][:, 0:1024]))
        B.W("tensor", "v", yfv)
        for oc in (0, 512):
            for k in range(8):
                B.T(lambda t, oc=oc, k=k: t.matmul(P[1][:, oc:oc + 512],
                                                   S["yfm"][:, k * 128:(k + 1) * 128],
                                                   S["wo"][:, k * H + oc:k * H + oc + 512],
                                                   start=(k == 0), stop=False))
            B.T(lambda t, oc=oc: t.matmul(P[1][:, oc:oc + 512], S["onesrow"][:],
                                          S["bout_row"][:, oc:oc + 512], start=False, stop=True))
        two = B.c["t"]
        B.W("vector", "t", two)
        zv = B.V(lambda v: v.tensor_tensor(S["zt"][:], P[1][:, 0:1024], S["xln"][:], ALU.add))
        B.W("scalar", "v", zv)
        nzv = ln_stats_apply("zt", "scr1")
        B.V(lambda v: v.tensor_tensor(S["scr2"][:], S["scr1"][:], S["lngbc"][:], ALU.mult))
        B.V(lambda v: v.tensor_tensor(S["scr1"][:], S["scr2"][:], S["lnbbc"][:], ALU.add))
        x1v = B.V(lambda v: v.tensor_tensor(S["xt"][:], S["xt"][:], S["scr1"][:], ALU.add))
        B.W("sync", "v", x1v)
        B.D("dxo", lambda e: e.dma_start(dr["x1_s"][b, t0:t0 + C, :], S["xt"][:]))
        if PHASES == "attn":
            B.D("dou", lambda e: e.dma_start(dr["out"][b, t0:t0 + C, :], S["xt"][:]))
        if DEBUG:
            B.D("ddb", lambda e: e.dma_start(dr["d_x1"][ci], S["xt"][:]))

        if STAGE < 10:
            marks[("cidone", ci)] = B.c["v"]
            return
        B.W("scalar", "v", x1v)
        z2v = ln_stats_apply("xt", "xln")
        B.W("tensor", "v", z2v)
        ex2 = fm_transpose("xln", "xlnfm")
        B.W("sync", "v", ex2)
        for fc in range(8):
            B.D("dzo", lambda e, fc=fc: e.dma_start(dr["zlns"][b, fc, :, t0:t0 + C],
                                                    S["xlnfm"][:, fc * 128:(fc + 1) * 128]))
        B.W("tensor", "v", ex2)
        for ho in range(8):
            for k in range(8):
                B.T(lambda t, ho=ho, k=k: t.matmul(P[0][:, ho * 128:(ho + 1) * 128],
                                                   S["gw1"][:, k * H + ho * 128:k * H + ho * 128 + 128],
                                                   S["xlnfm"][:, k * 128:(k + 1) * 128],
                                                   start=(k == 0), stop=(k == 7)))
        tg1 = B.c["t"]
        B.W("scalar", "t", tg1)
        for ho in range(8):
            B.Sc(lambda s, ho=ho: s.activation(S["hg"][:, ho * 128:(ho + 1) * 128],
                                               P[0][:, ho * 128:(ho + 1) * 128], AF.Relu,
                                               bias=S["gb1c"][:, ho:ho + 1]))
        sg1 = B.c["s"]
        if DEBUG and ci in (0, 16):
            B.W("sync", "s", sg1)
            B.D("ddb", lambda e, ci=ci: e.dma_start(dr["d_hg"][ci // 16], S["hg"][:]))
        B.W("tensor", "s", sg1)
        for k in range(8):
            B.T(lambda t, k=k: t.matmul(P[4][0:8, 256:384], S["gw2"][:, k * 8:(k + 1) * 8],
                                        S["hg"][:, k * 128:(k + 1) * 128],
                                        start=(k == 0), stop=(k == 7)))
        tg2 = B.c["t"]
        B.W("vector", "t", tg2)
        B.V(lambda v, b=b: v.tensor_tensor(S["gacc"][:, b * 128:(b + 1) * 128],
                                           S["gacc"][:, b * 128:(b + 1) * 128],
                                           P[4][0:8, 256:384], ALU.add))
        if DEBUG:
            dgc = B.V(lambda v: v.tensor_copy(S["scrg"][0:8, :], P[4][0:8, 256:384]))
            B.W("sync", "v", dgc)
            B.D("ddb", lambda e, ci=ci: e.dma_start(dr["d_gacc"][ci], S["scrg"][0:8, :]))
        marks[("cidone", ci)] = B.c["v"]

    def plan_ffn():
        ae = marks["attn_end"]
        for e in ("tensor", "vector", "scalar", "gpsimd", "sync"):
            B.W(e, "t", ae["t"]); B.W(e, "v", ae["v"]); B.W(e, "s", ae["s"])
            for k in Bld.DKEYS:
                B.W(e, k, ae[k])

        dtb2 = B.D("dc", lambda e: e.dma_start(S["xtf"][0:8, :], dr["tb2"]))
        db2r = B.D("dc", lambda e: e.dma_start(S["scrF"][0:1, :], dr["b2row"]))
        for b in range(NB):
            B.V(lambda v, b=b: v.tensor_reduce(S["gcol"][:, b:b + 1], S["gacc"][:, b * 128:(b + 1) * 128],
                                               AX, ALU.add))
            B.V(lambda v, b=b: v.tensor_scalar(S["gcol"][:, b:b + 1], S["gcol"][:, b:b + 1],
                                               1.0 / T, S["gb2c"][:], ALU.mult, ALU.add))
        gv = B.c["v"]
        if DEBUG:
            B.W("sync", "v", gv)
            B.D("ddb", lambda e: e.dma_start(dr["d_g"], S["gcol"][:]))
        if FSUB < 2:
            return
        B.W("tensor", "v", gv)
        for b in range(NB):
            B.T(lambda t, b=b: t.transpose(P[4][0:1, 240 + 8 * b:248 + 8 * b],
                                           S["gcol"][:, b:b + 1], S["identf"][0:8, 0:8]))
        tgr = B.c["t"]
        B.W("vector", "t", tgr)
        grv = B.V(lambda v: v.tensor_copy(S["grow"][0:1, 0:16], P[4][0:1, 240:256]))
        B.W("tensor", "v", grv)
        tgb = B.T(lambda t: t.matmul(P[4][:, 0:16], S["onesrow"][:], S["grow"][0:1, 0:16], start=True, stop=True))
        B.W("vector", "t", tgb)
        gbv = B.V(lambda v: v.tensor_copy(S["gbc"][:], P[4][:, 0:16]))
        if FSUB < 3:
            return
        B.W("tensor", "v", gbv)
        for fcch in range(32):
            B.W("sync", "t", marks.get(("tb1t", fcch - 1), 0))
            B.D("dtb", lambda e, fcch=fcch: e.dma_start(S["tb1st"][:], dr["tb1"][:, fcch * 128:(fcch + 1) * 128]))
            dmb = B.c["dtb"]
            B.W("tensor", "dtb", dmb)
            marks[("tb1t", fcch)] = B.T(lambda t, fcch=fcch: t.matmul(
                P[5][:, fcch * 2:fcch * 2 + 2], S["tb1st"][:], S["gcol"][:], start=True, stop=True))
        tmb1 = B.c["t"]
        B.W("vector", "t", tmb1)
        b1b = B.ap("b1c", [[1, 128], [1, 32], [0, 2]], off=0)
        B.V(lambda v, b1b=b1b: v.tensor_tensor(S["mb1c"][:], P[5][:, 0:64], b1b, ALU.add))
        if DEBUG:
            dmb1 = B.c["v"]
            B.W("sync", "v", dmb1)
            B.D("ddb", lambda e: e.dma_start(dr["d_mb1"], S["mb1c"][:]))
        if FSUB < 4:
            return
        B.W("tensor", "dc", db2r)
        B.W("vector", "dc", db2r)
        for b in range(NB):
            for half in range(2):
                B.W("tensor", "v", B.c["v"])
                tm2 = B.T(lambda t, b=b, half=half: t.matmul(
                    P[2][0:1, 0:512], S["gcol"][:, b:b + 1],
                    S["xtf"][0:8, half * 512:(half + 1) * 512], start=True, stop=True))
                B.W("vector", "t", tm2)
                B.V(lambda v, b=b, half=half: v.tensor_tensor(
                    S["mb2row"][0:1, b * H + half * 512:b * H + half * 512 + 512],
                    P[2][0:1, 0:512],
                    S["scrF"][0:1, half * 512:(half + 1) * 512], ALU.add))
        mb2v = B.c["v"]

        if FSTAGE < 2:
            return
        NSLOT = 2
        stage_seq = []
        for mat in (1, 2):
            nhc, nfb = (8, 8) if mat == 1 else (32, 2)
            for hc in range(nhc):
                for fb in range(nfb):
                    stage_seq.append((mat, hc, fb))
        vdone = {}
        for si, (mat, hc, fb) in enumerate(stage_seq):
            wT, twT, dramb1 = ("w1T", "tw1T", "mw1b1") if mat == 1 else ("w2T", "tw2T", "mw2b1")
            slot = si % NSLOT
            B.W("gpsimd", "v", vdone.get(si - NSLOT, mb2v))
            B.DG(lambda e, wT=wT, hc=hc, fb=fb, slot=slot:
                 e.dma_start(S["wst"][:, slot * 512:slot * 512 + 512],
                             dr[wT][hc * 128:(hc + 1) * 128, fb * 512:(fb + 1) * 512]))
            for e8 in range(E):
                B.DG(lambda e, twT=twT, e8=e8, hc=hc, fb=fb, slot=slot:
                     e.dma_start(S["twst"][:, (slot * E + e8) * 512:(slot * E + e8) * 512 + 512],
                                 dr[twT][e8, hc * 128:(hc + 1) * 128, fb * 512:(fb + 1) * 512]))
            dall = B.c["dg"]
            B.W("vector", "dg", dall)
            for b in range(NB):
                if b == 0:
                    dsl = (S["mw1"][:, hc * F + fb * 512:hc * F + fb * 512 + 512] if mat == 1
                           else S["mw2"][:, hc * H + fb * 512:hc * H + fb * 512 + 512])
                else:
                    dsl = S["scrW"][:, 0:512]
                    B.W("vector", "dms", B.c["dms"])
                for e8 in range(E):
                    src1 = S["wst"][:, slot * 512:slot * 512 + 512] if e8 == 0 else dsl
                    B.V(lambda v, dsl=dsl, e8=e8, b=b, slot=slot, src1=src1:
                        v.scalar_tensor_tensor(dsl,
                                               S["twst"][:, (slot * E + e8) * 512:(slot * E + e8) * 512 + 512],
                                               S["gbc"][:, b * 8 + e8:b * 8 + e8 + 1],
                                               src1, ALU.mult, ALU.add))
                if b == 1:
                    vw = B.c["v"]
                    B.W("sync", "v", vw)
                    B.D("dms", lambda e, hc=hc, fb=fb, dramb1=dramb1:
                        e.dma_start(dr[dramb1][hc, :, fb * 512:(fb + 1) * 512], S["scrW"][:, 0:512]))
            vdone[si] = B.c["v"]
        f0_v = B.c["v"]
        marks["dms_total"] = B.c["dms"]

        if FSTAGE < 3:
            return
        mb1v_total = B.c["v"]
        B.W("scalar", "v", mb1v_total)
        for b in range(NB):
            if b == 1:
                B.W("sync", "t", marks[("f1tend", 0)])
                B.W("sync", "dms", marks["dms_total"])
                for hc in range(8):
                    B.D("dmw", lambda e, hc=hc: e.dma_start(S["mw1"][:, hc * F:(hc + 1) * F], dr["mw1b1"][hc]))
                for fcch in range(32):
                    B.D("dmw", lambda e, fcch=fcch: e.dma_start(S["mw2"][:, fcch * H:(fcch + 1) * H], dr["mw2b1"][fcch]))
                mwdx = B.c["dmw"]
                B.W("tensor", "dmw", mwdx)
            B.W("tensor", "v", mb2v if b == 0 else marks[("fbdone", 0)])
            for half in range(2):
                B.T(lambda t, b=b, half=half: t.matmul(
                    P[4][:, 0:512] if half == 0 else P[5][:, 0:512], S["onesrow"][:],
                    S["mb2row"][0:1, b * H + half * 512:b * H + half * 512 + 512], start=True, stop=True))
            tmb2 = B.c["t"]
            B.W("vector", "t", tmb2)
            B.V(lambda v: v.tensor_copy(S["mb2bc"][:, 0:512], P[4][:, 0:512]))
            mbbv = B.V(lambda v: v.tensor_copy(S["mb2bc"][:, 512:1024], P[5][:, 0:512]))

            for w in range(NW):
                wt0 = w * WIN
                B.W("sync", "t", marks.get(("f1t", b, w - 1), 0))
                for fcch in range(8):
                    B.D("dzr", lambda e, b=b, fcch=fcch, wt0=wt0:
                        e.dma_start(S["zrh"][:, fcch * WIN:(fcch + 1) * WIN],
                                    dr["zlns"][b, fcch, :, wt0:wt0 + WIN]))
                dzl = B.c["dzr"]
                B.W("tensor", "dzr", dzl)
                if b == 0 and w == 0:
                    B.W("tensor", "v", f0_v)
                B.W("scalar", "t", marks.get(("f2t", b, w - 1), 0))
                for fo in range(32):
                    pi = fo % 2
                    B.W("tensor", "s", marks.get(("gelu", b, w, fo - 2), 0))
                    for k in range(8):
                        B.T(lambda t, fo=fo, k=k, pi=pi: t.matmul(
                            P[pi][:, 0:WIN],
                            S["mw1"][:, k * F + fo * 128:k * F + fo * 128 + 128],
                            S["zrh"][:, k * WIN:(k + 1) * WIN],
                            start=(k == 0), stop=(k == 7)))
                    tf1 = B.c["t"]
                    B.W("scalar", "t", tf1)
                    sg = B.Sc(lambda s, fo=fo, pi=pi, b=b: s.activation(
                        S["u"][:, fo * WIN:(fo + 1) * WIN], P[pi][:, 0:WIN], AF.Gelu_apprx_tanh,
                        bias=S["mb1c"][:, fo * 2 + b:fo * 2 + b + 1]))
                    marks[("gelu", b, w, fo)] = sg
                marks[("f1t", b, w)] = B.c["t"]
                if DEBUG and b == 0 and w == 0:
                    B.W("sync", "s", marks[("gelu", 0, 0, 3)])
                    for fo in range(4):
                        B.D("ddb", lambda e, fo=fo: e.dma_start(dr["d_u"][fo], S["u"][:, fo * WIN:(fo + 1) * WIN]))
                for tt in range(WIN // 128):
                    dx1 = B.D("dxf", lambda e, b=b, wt0=wt0, tt=tt:
                              e.dma_start(S["xtf"][:], dr["x1_s"][b, wt0 + tt * 128:wt0 + tt * 128 + 128, :]))
                    B.W("tensor", "s", marks[("gelu", b, w, 31)])
                    B.W("tensor", "v", marks.get(("f2o", b, w, tt - 1), 0))
                    for ho in range(2):
                        pi = 2 + ho
                        for k in range(32):
                            B.T(lambda t, tt=tt, ho=ho, k=k, pi=pi: t.matmul(
                                P[pi][:, 0:512],
                                S["u"][:, k * WIN + tt * 128:k * WIN + tt * 128 + 128],
                                S["mw2"][:, k * H + ho * 512:k * H + ho * 512 + 512],
                                start=(k == 0), stop=(k == 31)))
                    tf2 = B.c["t"]
                    B.W("vector", "t", tf2)
                    B.W("vector", "dxf", dx1)
                    B.W("vector", "v", mbbv)
                    B.W("vector", "dou", B.c["dou"])
                    B.V(lambda v: v.tensor_tensor(S["scrF"][:, 0:512], P[2][:, 0:512], S["xtf"][:, 0:512], ALU.add))
                    B.V(lambda v: v.tensor_tensor(S["scrF"][:, 512:1024], P[3][:, 0:512],
                                                  S["xtf"][:, 512:1024], ALU.add))
                    fv = B.V(lambda v: v.tensor_tensor(S["scrF"][:], S["scrF"][:], S["mb2bc"][:], ALU.add))
                    B.W("sync", "v", fv)
                    B.D("dou", lambda e, b=b, wt0=wt0, tt=tt: e.dma_start(
                        dr["out"][b, wt0 + tt * 128:wt0 + tt * 128 + 128, :], S["scrF"][:]))
                    marks[("f2o", b, w, tt)] = B.c["v"]
                marks[("f2t", b, w)] = B.c["t"]
            marks[("f1tend", b)] = B.c["t"]
            marks[("fbdone", b)] = B.c["v"]
        for k in Bld.DKEYS:
            if k != "dg":
                B.W("sync", k, B.c[k])
        B.W("gpsimd", "dg", B.c["dg"])
    def plan_ffn_outer():
        plan_ffn()
        if FSTAGE < 99:
            for e in ("tensor", "vector", "scalar", "gpsimd", "sync"):
                B.W(e, "t", B.c["t"]); B.W(e, "v", B.c["v"]); B.W(e, "s", B.c["s"])
                for k in Bld.DKEYS:
                    B.W(e, k, B.c[k])

    def run_pass(eng, eb, which):
        B.eng, B.eb = eng, eb
        if which == "attn":
            B.reset()
            plan_attn()
            marks["snap_attn"] = dict(B.c)
        else:
            B.reset(marks["snap_attn"])
            if PHASES == "all":
                plan_ffn_outer()
            else:
                for e in ("tensor", "vector", "scalar", "gpsimd", "sync"):
                    ae = marks["attn_end"]
                    B.W(e, "t", ae["t"]); B.W(e, "v", ae["v"]); B.W(e, "s", ae["s"])
                    for k in Bld.DKEYS:
                        B.W(e, k, ae[k])

    with contextlib.ExitStack() as s0:
        for nm, sh, dt in (
            ("identf", (128, 128), F32), ("ltri", (128, 128), F32),
            ("tmask4", (128, 512), BF16), ("onescol", (128, 1), F32),
            ("onesrow", (1, 128), F32), ("econst", (64, 128), F32),
            ("msel", (64, 16), F32), ("hmask", (128, 64), F32),
            ("cos_t", (128, NCHUNK * 16), F32),
            ("sin_t", (128, NCHUNK * 16), F32), ("bbm_row", (1, 128), F32),
            ("bout_row", (1, H), F32), ("lngbc", (128, H), F32),
            ("lnbbc", (128, H), F32), ("gb1c", (128, 8), F32),
            ("gb2c", (8, 1), F32), ("b1c", (128, 32), F32),
            ("stat", (128, 16), F32), ("mbstat", (128, 16), F32),
            ("gacc", (8, NB * 128), F32), ("gcol", (8, NB), F32),
            ("grow", (1, 16), F32), ("gbc", (128, 8 * NB), F32),
            ("mb1c", (128, 64), F32), ("mb2row", (1, NB * H), F32),
            ("scrg", (8, 128), F32),
        ):
            B.sb(s0, nm, sh, dt)
        if DEBUG:
            B.sb(s0, "dbg", (128, 140), F32)
        for pi_, wdt in enumerate((1024, 1024, 512, 512, 512, 512)):
            P.append(s0.enter_context(nc.psum_tensor("ps%d" % pi_, [128, wdt], F32)))
        for n in ("t", "v", "s", "g") + Bld.DKEYS:
            B.sem[n] = s0.enter_context(nc.semaphore("sem_" + n))

        with contextlib.ExitStack() as sA:
            for nm, sh, dt in (
                ("wq", (128, 8 * H), BF16), ("wk", (128, 8 * H), BF16),
                ("wv", (128, 8 * H), BF16), ("wo", (128, 8 * H), BF16),
                ("wbm", (128, 8 * 128), BF16), ("gw1", (128, 8 * H), BF16),
                ("gw2", (128, 8 * 8), BF16),
                ("xt", (128, H), F32), ("xln", (128, H), F32),
                ("xlnfm", (128, 8 * 128), BF16),
                ("rotq", (128, H), F32), ("rotk", (128, H), F32),
                ("phiq", (128, H), F32), ("phik", (128, H), F32),
                ("scr1", (128, H), F32), ("scr2", (128, H), F32),
                ("phiqm", (128, 4 * 8 * 128), BF16), ("phikf", (128, 8 * 128), BF16),
                ("vstg", (128, NH * 33), BF16), ("vs", (128, NH * 66), BF16),
                ("ks", (128, M * H), BF16), ("bmt", (128, 128), F32),
                ("pip", (128, 64), F32), ("pp", (128, 64), F32),
                ("mixw", (128, 64), F32), ("plastc", (128, 32), F32),
                ("gsb", (128, 4 * 128), BF16),
                ("Tma", (128, NB * 16 * 132), F32), ("Sbf", (128, NB * 16 * 132), BF16),
                ("yt", (128, H), F32), ("yfm", (128, 8 * 128), BF16),
                ("zt", (128, H), F32), ("hg", (128, 8 * 128), BF16),
            ):
                B.sb(sA, nm, sh, dt)
            with nc.Block() as blkA:
                @blkA.tensor
                def _(t):
                    run_pass("tensor", t, "attn")

                @blkA.vector
                def _(v):
                    run_pass("vector", v, "attn")

                @blkA.scalar
                def _(s):
                    run_pass("scalar", s, "attn")

                @blkA.gpsimd
                def _(g):
                    run_pass("gpsimd", g, "attn")

                @blkA.sync
                def _(sy):
                    run_pass("sync", sy, "attn")

        with contextlib.ExitStack() as sB:
            for nm, sh, dt in (
                ("mw1", (128, 8 * F), BF16), ("mw2", (128, 32 * H), BF16),
                ("u", (128, 32 * WIN), BF16), ("zrh", (128, 8 * WIN), BF16),
                ("wst", (128, 2 * 512), BF16), ("twst", (128, 2 * E * 512), BF16),
                ("scrW", (128, 512), BF16), ("xtf", (128, H), F32),
                ("scrF", (128, H), F32), ("mb2bc", (128, H), F32),
                ("tb1st", (8, 128), F32),
            ):
                B.sb(sB, nm, sh, dt)
            with nc.Block() as blkB:
                @blkB.tensor
                def _(t):
                    run_pass("tensor", t, "ffn")

                @blkB.vector
                def _(v):
                    run_pass("vector", v, "ffn")

                @blkB.scalar
                def _(s):
                    run_pass("scalar", s, "ffn")

                @blkB.gpsimd
                def _(g):
                    run_pass("gpsimd", g, "ffn")

                @blkB.sync
                def _(sy):
                    run_pass("sync", sy, "ffn")

    nc.finalize()
    return nc


_NC_CACHE = None


def kernel(**inputs):
    global _NC_CACHE
    weights = make_weight_map(inputs)
    if _NC_CACHE is None:
        _NC_CACHE = build_program()
    x = np.asarray(inputs["L0_attn_x"])
    in_maps = []
    for c in range(8):
        m = dict(weights)
        m["x"] = np.ascontiguousarray(x[2 * c:2 * c + 2])
        in_maps.append(m)
    res = run_bass_kernel_spmd(_NC_CACHE, in_maps, core_ids=list(range(8)))
    return np.concatenate([r["out"] for r in res.results], axis=0)


# version 06
# speedup vs baseline: 1.2155x; 1.0015x over previous
import contextlib
import numpy as np
import ml_dtypes
import concourse.bass as bass
import concourse.mybir as mybir
from concourse.bass_utils import run_bass_kernel_spmd

F32 = mybir.dt.float32
BF16 = mybir.dt.bfloat16
AF = mybir.ActivationFunctionType
ALU = mybir.AluOpType
BF = ml_dtypes.bfloat16
AX = mybir.AxisListType.X

NB, T, C, H, NH, D, M, F, E = 2, 2048, 128, 1024, 32, 32, 2, 4096, 8
NCHUNK = T // C
LN_EPS = 1e-5
DEN_EPS = 1e-6
BMIN, BMAX = 0.85, 0.9995
WIN = 256
NW = T // WIN

DEBUG = False
PHASES = "all"
STAGE = 99
NCI = 99
SUB = 99
FSTAGE = 99
FSUB = 99


def host_consts():
    c = {}
    c["identf"] = np.eye(128, dtype=np.float32)
    jj, ii = np.meshgrid(np.arange(128), np.arange(128), indexing="ij")
    c["ltri"] = (jj <= ii).astype(np.float32)
    c["tmask4"] = np.tile((jj <= ii).astype(BF), (1, 4))
    c["onescol"] = np.ones((128, 1), np.float32)
    hm = np.zeros((128, 64), np.float32)
    for p in range(128):
        hm[p, p // 32] = 1.0
    c["hmask"] = hm
    c["onesrow"] = np.ones((1, 128), np.float32)
    ee = np.zeros((64, 128), np.float32)
    for r in range(64):
        ee[r, ((r // 2) % 4) * 32:((r // 2) % 4) * 32 + 32] = 1.0
    c["econst"] = ee
    ms = np.zeros((64, 16), np.float32)
    for r in range(64):
        ms[r, 2 * ((r // 2) // 4) + (r % 2)] = 1.0
    c["msel"] = ms
    half = D // 2
    inv_freq = (1.0 / (10000.0 ** (np.arange(half, dtype=np.float32) / np.float32(half)))).astype(np.float32)
    t = np.arange(T, dtype=np.float32)
    freqs = (t[:, None] * inv_freq[None, :]).astype(np.float32)
    cos = np.cos(freqs).astype(np.float32)
    sin = np.sin(freqs).astype(np.float32)
    c["cos_t"] = cos.reshape(NCHUNK, 128, half).transpose(1, 0, 2).reshape(128, NCHUNK * half).copy()
    c["sin_t"] = sin.reshape(NCHUNK, 128, half).transpose(1, 0, 2).reshape(128, NCHUNK * half).copy()
    return c


_CONSTS = host_consts()


def make_weight_map(inputs):
    g = lambda k: np.asarray(inputs[k])
    w = {}
    w["wqT"] = np.ascontiguousarray(g("L0_attn_Wq").T)
    w["wkT"] = np.ascontiguousarray(g("L0_attn_Wk").T)
    w["wvT"] = np.ascontiguousarray(g("L0_attn_Wv").T)
    w["woutT"] = np.ascontiguousarray(g("L0_attn_Wout").T)
    w["wbmT"] = np.ascontiguousarray(np.concatenate([g("L0_attn_Wbeta"), g("L0_attn_Wmix")], 0).T)
    w["bbm_row"] = np.ascontiguousarray(np.concatenate([g("L0_attn_bbeta"), g("L0_attn_bmix")])[None, :])
    w["bout_row"] = np.ascontiguousarray(g("L0_attn_bout")[None, :])
    w["lng_row"] = np.ascontiguousarray(g("L0_attn_ln_g")[None, :])
    w["lnb_row"] = np.ascontiguousarray(g("L0_attn_ln_b")[None, :])
    w["gw1T"] = np.ascontiguousarray(g("L0_ffn_gate_w1").T)
    w["gb1c"] = np.ascontiguousarray(g("L0_ffn_gate_b1").reshape(8, 128).T)
    w["gw2T"] = np.ascontiguousarray(g("L0_ffn_gate_w2").T)
    w["gb2c"] = np.ascontiguousarray(g("L0_ffn_gate_b2").reshape(8, 1))
    w["w1T"] = np.ascontiguousarray(g("L0_ffn_W1").T)
    w["tw1T"] = np.ascontiguousarray(g("L0_ffn_TW1").transpose(0, 2, 1))
    w["b1c"] = np.ascontiguousarray(g("L0_ffn_b1").reshape(32, 128).T)
    w["tb1"] = np.ascontiguousarray(g("L0_ffn_Tb1"))
    w["w2T"] = np.ascontiguousarray(g("L0_ffn_W2").T)
    w["tw2T"] = np.ascontiguousarray(g("L0_ffn_TW2").transpose(0, 2, 1))
    w["b2row"] = np.ascontiguousarray(g("L0_ffn_b2")[None, :])
    w["tb2"] = np.ascontiguousarray(g("L0_ffn_Tb2"))
    w.update(_CONSTS)
    return w


class Bld:
    def __init__(self, nc):
        self.nc = nc
        self.dr = {}
        self.S = {}
        self.P = []
        self.sem = {}
        self.eng = None
        self.eb = None
        self.marks = {}

    def din(self, name, shape, dt=F32):
        self.dr[name] = self.nc.dram_tensor(name, list(shape), dt, kind="ExternalInput").ap()

    def dout(self, name, shape, dt=F32):
        self.dr[name] = self.nc.dram_tensor(name, list(shape), dt, kind="ExternalOutput").ap()

    def dint(self, name, shape, dt=F32):
        self.dr[name] = self.nc.dram_tensor(name, list(shape), dt, kind="Internal").ap()

    def sb(self, stack, name, shape, dt=F32):
        self.S[name] = stack.enter_context(self.nc.sbuf_tensor("s_" + name, list(shape), dt))

    DKEYS = ("dc", "dxi", "dzo", "dxo", "dou", "dzr", "dxf", "dmw", "dms", "dtb", "ddb", "dg")

    def reset(self, snap=None):
        base = dict(t=0, v=0, s=0, g=0)
        base.update({k: 0 for k in self.DKEYS})
        self.c = dict(snap) if snap else base

    SELFWAIT = ("v", "s", "g")

    def _op(self, key, which, fn, n=1):
        if key in self.SELFWAIT and self.eng == which and self.c[key] > 0:
            self.eb.wait_ge(self.sem[key], self.c[key])
        self.c[key] += n
        if self.eng == which:
            fn(self.eb).then_inc(self.sem[key], n)
        return self.c[key]

    def T(self, fn):
        return self._op("t", "tensor", fn)

    def V(self, fn):
        return self._op("v", "vector", fn)

    def Sc(self, fn):
        return self._op("s", "scalar", fn)

    def D(self, key, fn):
        return self._op(key, "sync", fn, 16)

    def DG(self, fn):
        return self._op("dg", "gpsimd", fn, 16)

    def W(self, eng, key, val):
        if self.eng == eng and val > 0:
            self.eb.wait_ge(self.sem[key], val)

    def ap(self, name, dims, off=0):
        a = self.S[name][:]
        pitch = a.ap[0][0]
        dd = [list(d) for d in dims]
        dd[0] = [pitch, dd[0][1]]
        return bass.AP(a.tensor, off, dd)

    def app(self, ps_idx, dims, off=0):
        a = self.P[ps_idx][:]
        pitch = a.ap[0][0]
        dd = [list(d) for d in dims]
        dd[0] = [pitch, dd[0][1]]
        return bass.AP(a.tensor, off, dd)


def build_program():
    nc = bass.Bass("TRN2", target_bir_lowering=False, detect_race_conditions=False)
    B = Bld(nc)
    S, P, dr, marks = B.S, B.P, B.dr, B.marks

    B.din("x", (NB, T, H))
    for n in ("wqT", "wkT", "wvT", "woutT"):
        B.din(n, (H, H))
    B.din("wbmT", (H, 128)); B.din("bbm_row", (1, 128))
    B.din("bout_row", (1, H)); B.din("lng_row", (1, H)); B.din("lnb_row", (1, H))
    B.din("gw1T", (H, H)); B.din("gb1c", (128, 8)); B.din("gw2T", (H, 8)); B.din("gb2c", (8, 1))
    B.din("w1T", (H, F)); B.din("tw1T", (E, H, F)); B.din("b1c", (128, 32)); B.din("tb1", (E, F))
    B.din("w2T", (F, H)); B.din("tw2T", (E, F, H)); B.din("b2row", (1, H)); B.din("tb2", (E, H))
    B.din("identf", (128, 128)); B.din("ltri", (128, 128)); B.din("tmask4", (128, 512), BF16)
    B.din("onescol", (128, 1)); B.din("onesrow", (1, 128)); B.din("hmask", (128, 64))
    B.din("econst", (64, 128)); B.din("msel", (64, 16))
    B.din("cos_t", (128, NCHUNK * 16)); B.din("sin_t", (128, NCHUNK * 16))

    B.dout("out", (NB, T, H))
    B.dint("x1_s", (NB, T, H))
    B.dint("zlns", (NB, 8, 128, T), BF16)
    B.dint("mw1b1", (8, 128, F), BF16)
    B.dint("mw2b1", (32, 128, H), BF16)
    if DEBUG:
        for n, sh in (("d_xln", (NB * NCHUNK, 128, H)), ("d_phiq", (NB * NCHUNK, 128, H)),
                      ("d_phik", (NB * NCHUNK, 128, H)),
                      ("d_pip", (NB * NCHUNK, 128, 128)), ("d_y", (NB * NCHUNK, 128, H)),
                      ("d_x1", (NB * NCHUNK, 128, H)), ("d_g", (8, NB)),
                      ("d_mb1", (128, 64))):
            B.dout(n, sh)
        B.dout("d_v", (NB * NCHUNK, 128, NH * 33), BF16)
        B.dout("d_u", (4, 128, WIN), BF16)
        B.dout("d_hg", (2, 128, 8 * 128), BF16)
        B.dout("d_chain", (6, 128, 64))
        B.dout("d_s8", (6, 128, 140))
        B.dout("d_gacc", (NB * NCHUNK, 8, 128))

    def plan_attn():
        for n in ("identf", "ltri", "onescol", "onesrow", "econst", "msel",
                  "cos_t", "sin_t", "bbm_row", "bout_row", "gb1c", "gb2c", "b1c",
                  "tmask4", "hmask"):
            B.D("dc", lambda e, n=n: e.dma_start(S[n][:], dr[n]))
        B.D("dc", lambda e: e.dma_start(S["lngbc"][0:1, :], dr["lng_row"]))
        cdx = B.D("dc", lambda e: e.dma_start(S["lnbbc"][0:1, :], dr["lnb_row"]))
        B.W("tensor", "dc", cdx)
        for half in range(2):
            B.T(lambda t, half=half: t.matmul(P[0][:, half * 512:(half + 1) * 512], S["onesrow"][:],
                                              S["lngbc"][0:1, half * 512:(half + 1) * 512],
                                              start=True, stop=True))
            B.T(lambda t, half=half: t.matmul(P[1][:, half * 512:(half + 1) * 512], S["onesrow"][:],
                                              S["lnbbc"][0:1, half * 512:(half + 1) * 512],
                                              start=True, stop=True))
        tb = B.c["t"]
        B.W("vector", "t", tb)
        B.V(lambda v: v.tensor_copy(S["lngbc"][:], P[0][:, 0:1024]))
        lnv = B.V(lambda v: v.tensor_copy(S["lnbbc"][:], P[1][:, 0:1024]))
        marks["lnv"] = lnv
        for (wname, dname, width) in (("wq", "wqT", H), ("wk", "wkT", H), ("wv", "wvT", H),
                                      ("wo", "woutT", H), ("wbm", "wbmT", 128),
                                      ("gw1", "gw1T", H), ("gw2", "gw2T", 8)):
            for k in range(8):
                B.DG(lambda e, wname=wname, dname=dname, width=width, k=k:
                     e.dma_start(S[wname][:, k * width:(k + 1) * width],
                                 dr[dname][k * 128:(k + 1) * 128, :]))
        marks["wdg"] = B.c["dg"]
        marks["cdx"] = cdx
        B.W("vector", "dc", cdx)
        if DEBUG:
            dv0 = B.V(lambda v: v.tensor_copy(S["dbg"][:, 0:4], S["hmask"][:, 0:4]))
            B.W("sync", "v", dv0)
            B.D("ddb", lambda e: e.dma_start(dr["d_s8"][3][:, 70:74], S["dbg"][:, 0:4]))
        B.V(lambda v: v.memset(S["Tma"][:], 0.0))
        B.V(lambda v: v.memset(S["Sbf"][:], 0.0))
        B.V(lambda v: v.memset(S["vstg"][:], 0.0))
        B.V(lambda v: v.memset(B.ap("vstg", [[1, 128], [33, NH], [1, 1]], off=32), 1.0))
        B.V(lambda v: v.memset(S["plastc"][:], 1.0))
        B.V(lambda v: v.memset(S["gacc"][:], 0.0))

        nci_done = 0
        for b in range(NB):
            for cc in range(NCHUNK):
                if nci_done < NCI:
                    attn_ci(b, cc)
                    nci_done += 1
        marks["attn_end"] = dict(B.c)

    def ln_stats_apply(src, dst):
        B.Sc(lambda s: s.activation(S["scr1"][:], S[src][:], AF.Copy, accum_out=S["stat"][:, 0:1]))
        s2 = B.Sc(lambda s: s.activation(S["scr2"][:], S[src][:], AF.Square, accum_out=S["stat"][:, 1:2]))
        B.W("vector", "s", s2)
        B.V(lambda v: v.tensor_scalar(S["stat"][:, 2:3], S["stat"][:, 0:1], 1.0 / H, None, ALU.mult))
        B.V(lambda v: v.tensor_tensor(S["stat"][:, 3:4], S["stat"][:, 2:3], S["stat"][:, 2:3], ALU.mult))
        vv = B.V(lambda v: v.tensor_scalar(S["stat"][:, 4:5], S["stat"][:, 1:2], 1.0 / H, S["stat"][:, 3:4],
                                           ALU.mult, ALU.subtract))
        vv = B.V(lambda v: v.tensor_scalar(S["stat"][:, 4:5], S["stat"][:, 4:5], LN_EPS, None, ALU.add))
        B.W("scalar", "v", vv)
        B.Sc(lambda s: s.activation(S["stat"][:, 5:6], S["stat"][:, 4:5], AF.Ln))
        sr = B.Sc(lambda s: s.activation(S["stat"][:, 6:7], S["stat"][:, 5:6], AF.Exp, scale=-0.5))
        B.W("vector", "s", sr)
        B.V(lambda v: v.tensor_tensor(S["stat"][:, 7:8], S["stat"][:, 2:3], S["stat"][:, 6:7], ALU.mult))
        return B.V(lambda v: v.tensor_scalar(S[dst][:], S[src][:], S["stat"][:, 6:7], S["stat"][:, 7:8],
                                             ALU.mult, ALU.subtract))

    def fm_transpose(src, dst):
        for fc in range(8):
            pi, poff = (4, (fc % 4) * 128) if fc < 4 else (5, (fc % 4) * 128)
            B.T(lambda t, fc=fc, pi=pi, poff=poff:
                t.transpose(P[pi][:, poff:poff + 128], S[src][:, fc * 128:(fc + 1) * 128], S["identf"][:]))
        tt = B.c["t"]
        B.W("vector", "t", tt)
        B.V(lambda v: v.tensor_copy(S[dst][:, 0:512], P[4][:, 0:512]))
        return B.V(lambda v: v.tensor_copy(S[dst][:, 512:1024], P[5][:, 0:512]))

    def attn_ci(b, cc):
        ci = b * NCHUNK + cc
        t0 = cc * C

        B.W("sync", "v", marks.get(("cidone", ci - 1), 0))
        B.W("sync", "dxo", B.c["dxo"])
        dxv = B.D("dxi", lambda e: e.dma_start(S["xt"][:], dr["x"][b, t0:t0 + C, :]))
        B.W("scalar", "dxi", dxv)
        if ci == 0:
            B.W("scalar", "v", marks["lnv"])
        xlnv = ln_stats_apply("xt", "xln")
        if DEBUG:
            B.W("sync", "v", xlnv)
            B.D("ddb", lambda e: e.dma_start(dr["d_xln"][ci], S["xln"][:]))
        if STAGE < 2:
            marks[("cidone", ci)] = B.c["v"]
            return
        B.W("tensor", "v", xlnv)
        B.W("vector", "dzo", B.c["dzo"])
        exv = fm_transpose("xln", "xlnfm")
        if STAGE < 3:
            marks[("cidone", ci)] = B.c["v"]
            return
        B.W("tensor", "v", exv)
        if ci == 0:
            B.W("tensor", "dg", marks["wdg"])
        targets = [("wq", 0, 512, 0, 0, H), ("wq", 512, 512, 0, 512, H),
                   ("wk", 0, 512, 1, 0, H), ("wk", 512, 512, 1, 512, H),
                   ("wv", 0, 512, 2, 0, H), ("wv", 512, 512, 3, 0, H),
                   ("wbm", 0, 128, 4, 0, 128)]
        for k in range(8):
            for (w, oc0, width, pi, poff, wwidth) in targets:
                B.T(lambda t, w=w, oc0=oc0, width=width, pi=pi, poff=poff, k=k, wwidth=wwidth:
                    t.matmul(P[pi][:, poff:poff + width],
                             S["xlnfm"][:, k * 128:(k + 1) * 128],
                             S[w][:, k * wwidth + oc0:k * wwidth + oc0 + width],
                             start=(k == 0), stop=(k == 7 and not (w == "wbm"))))
        tproj = B.T(lambda t: t.matmul(P[4][:, 0:128], S["onesrow"][:], S["bbm_row"][:],
                                       start=False, stop=True))

        if STAGE < 4:
            marks[("cidone", ci)] = B.c["v"]
            return
        B.W("vector", "t", tproj)
        co = cc * 16
        for (src_pi, dst) in ((0, "rotq"), (1, "rotk")):
            u1 = B.app(src_pi, [[1, 128], [32, 32], [1, 16]], off=0)
            u2 = B.app(src_pi, [[1, 128], [32, 32], [1, 16]], off=16)
            cosb = B.ap("cos_t", [[1, 128], [0, 32], [1, 16]], off=co)
            sinb = B.ap("sin_t", [[1, 128], [0, 32], [1, 16]], off=co)
            o1 = B.ap(dst, [[1, 128], [32, 32], [1, 16]], off=0)
            o2 = B.ap(dst, [[1, 128], [32, 32], [1, 16]], off=16)
            B.V(lambda v, u2=u2, sinb=sinb: v.tensor_tensor(S["scr1"][:, 0:512], u2, sinb, ALU.mult))
            B.V(lambda v, u1=u1, cosb=cosb: v.tensor_tensor(S["scr2"][:, 0:512], u1, cosb, ALU.mult))
            B.V(lambda v, o1=o1: v.tensor_tensor(o1, S["scr2"][:, 0:512], S["scr1"][:, 0:512], ALU.subtract))
            B.V(lambda v, u1=u1, sinb=sinb: v.tensor_tensor(S["scr1"][:, 0:512], u1, sinb, ALU.mult))
            B.V(lambda v, u2=u2, cosb=cosb: v.tensor_tensor(S["scr2"][:, 0:512], u2, cosb, ALU.mult))
            B.V(lambda v, o2=o2: v.tensor_tensor(o2, S["scr2"][:, 0:512], S["scr1"][:, 0:512], ALU.add))
        vd1 = B.ap("vstg", [[1, 128], [33, 16], [1, 32]], off=0)
        vd2 = B.ap("vstg", [[1, 128], [33, 16], [1, 32]], off=16 * 33)
        B.V(lambda v, vd1=vd1: v.tensor_copy(vd1, B.app(2, [[1, 128], [32, 16], [1, 32]], off=0)))
        ve1 = B.V(lambda v, vd2=vd2: v.tensor_copy(vd2, B.app(3, [[1, 128], [32, 16], [1, 32]], off=0)))
        if DEBUG:
            B.W("sync", "v", ve1)
            B.D("ddb", lambda e: e.dma_start(dr["d_v"][ci], S["vstg"][:]))
        B.W("scalar", "t", tproj)
        se = B.Sc(lambda s: s.activation(S["bmt"][:, 0:64], P[4][:, 0:64], AF.Exp, scale=-1.0))
        B.W("vector", "s", se)
        B.V(lambda v: v.tensor_scalar(S["bmt"][:, 0:64], S["bmt"][:, 0:64], 1.0, None, ALU.add))
        B.V(lambda v: v.reciprocal(S["scr1"][:, 0:64], S["bmt"][:, 0:64]))
        bclip = B.V(lambda v: v.tensor_scalar(S["bmt"][:, 0:64], S["scr1"][:, 0:64], BMIN, BMAX,
                                              ALU.max, ALU.min))
        B.W("scalar", "v", bclip)
        slog = B.Sc(lambda s: s.activation(S["bmt"][:, 64:128], S["bmt"][:, 0:64], AF.Ln))
        vmix = B.V(lambda v: v.tensor_copy(S["scr2"][:, 0:64], P[4][:, 64:128]))
        a1 = B.ap("scr2", [[1, 128], [2, 32], [1, 1]], off=0)
        a2 = B.ap("scr2", [[1, 128], [2, 32], [1, 1]], off=1)
        B.V(lambda v, a1=a1, a2=a2: v.tensor_tensor(S["scr1"][:, 0:32], a1, a2, ALU.subtract))
        sm = B.Sc(lambda s: s.activation(S["scr1"][:, 32:64], S["scr1"][:, 0:32], AF.Exp, scale=-1.0))
        B.W("vector", "s", sm)
        B.V(lambda v: v.tensor_scalar(S["scr1"][:, 32:64], S["scr1"][:, 32:64], 1.0, None, ALU.add))
        B.V(lambda v: v.reciprocal(S["scr2"][:, 0:32], S["scr1"][:, 32:64]))
        mx1 = B.ap("mixw", [[1, 128], [2, 32], [1, 1]], off=0)
        mx2 = B.ap("mixw", [[1, 128], [2, 32], [1, 1]], off=1)
        B.V(lambda v, mx1=mx1: v.tensor_copy(mx1, S["scr2"][:, 0:32]))
        mixv = B.V(lambda v, mx2=mx2: v.tensor_scalar(mx2, S["scr2"][:, 0:32], -1.0, 1.0, ALU.mult, ALU.add))

        if STAGE < 5:
            marks[("cidone", ci)] = B.c["v"]
            return
        B.W("tensor", "s", slog)
        B.W("tensor", "v", vmix)
        B.T(lambda t: t.matmul(P[4][:, 128:192], S["ltri"][:], S["bmt"][:, 64:128], start=True, stop=True))
        tcl = B.T(lambda t: t.matmul(P[4][0:64, 192:193], S["bmt"][:, 64:128], S["onescol"][:],
                                     start=True, stop=True))
        B.W("scalar", "t", tcl)
        B.Sc(lambda s: s.activation(S["pp"][:], P[4][:, 128:192], AF.Exp))
        sip = B.Sc(lambda s: s.activation(S["pip"][:], P[4][:, 128:192], AF.Exp, scale=-1.0))
        if DEBUG:
            B.W("vector", "s", sip)
            dv1 = B.V(lambda v: v.tensor_copy(S["scr1"][:, 0:64], S["pp"][:]))
            dv2 = B.V(lambda v: v.tensor_copy(S["scr1"][:, 64:128], S["pip"][:]))
            B.W("sync", "v", dv2)
            B.D("ddb", lambda e: e.dma_start(dr["d_pip"][ci], S["scr1"][:, 0:128]))
        B.W("vector", "t", tcl)
        clap = B.app(4, [[1, 64], [0, 16]], off=192)
        mselv = B.V(lambda v, clap=clap: v.tensor_tensor(S["mbstat"][0:64, 0:16], S["msel"][:], clap, ALU.mult))
        B.W("tensor", "v", mselv)
        B.W("tensor", "s", sip)
        tpl = B.T(lambda t: t.matmul(P[4][:, 208:224], S["econst"][:], S["mbstat"][0:64, 0:16],
                                     start=True, stop=True))
        B.W("scalar", "t", tpl)
        slot = (ci % 2) * 16
        pslot = B.Sc(lambda s, slot=slot: s.activation(S["plastc"][:, slot:slot + 16], P[4][:, 208:224], AF.Exp))
        if DEBUG and ci == 0:
            B.W("vector", "s", pslot)
            srcs = [("bmt", 0, 64), ("bmt", 64, 64), (4, 128, 64), ("pp", 0, 64),
                    ("pip", 0, 64), ("plastc", slot, 16)]
            for i6, (nm, off, wd) in enumerate(srcs):
                B.W("vector", "ddb", B.c["ddb"])
                src = P[4][:, off:off + wd] if nm == 4 else S[nm][:, off:off + wd]
                dv_ = B.V(lambda v, src=src, wd=wd: v.tensor_copy(S["dbg"][:, 0:wd], src))
                B.W("sync", "v", dv_)
                B.D("ddb", lambda e, i6=i6, wd=wd: e.dma_start(dr["d_chain"][i6][:, 0:wd], S["dbg"][:, 0:wd]))

        if STAGE < 6:
            marks[("cidone", ci)] = B.c["v"]
            return
        vminq = B.V(lambda v: v.tensor_scalar_min(S["scr1"][:], S["rotq"][:], 0.0))
        B.W("scalar", "v", vminq)
        B.Sc(lambda s: s.activation(S["scr2"][:], S["scr1"][:], AF.Exp))
        sphi = B.Sc(lambda s: s.activation(S["phiq"][:], S["rotq"][:], AF.Relu))
        B.W("vector", "s", sphi)
        B.V(lambda v: v.tensor_tensor(S["phiq"][:], S["phiq"][:], S["scr2"][:], ALU.add))
        vmink = B.V(lambda v: v.tensor_scalar_min(S["scr1"][:], S["rotk"][:], 0.0))
        B.W("scalar", "v", vmink)
        B.Sc(lambda s: s.activation(S["scr2"][:], S["scr1"][:], AF.Exp))
        sphk = B.Sc(lambda s: s.activation(S["phik"][:], S["rotk"][:], AF.Relu))
        B.W("vector", "s", sphk)
        phkv = B.V(lambda v: v.tensor_tensor(S["phik"][:], S["phik"][:], S["scr2"][:], ALU.add))
        if DEBUG:
            B.W("sync", "v", phkv)
            B.D("ddb", lambda e: e.dma_start(dr["d_phiq"][ci], S["phiq"][:]))
            B.D("ddb", lambda e: e.dma_start(dr["d_phik"][ci], S["phik"][:]))
        B.W("tensor", "v", phkv)
        for fc in range(8):
            B.T(lambda t, fc=fc: t.transpose(P[0][:, fc * 128:(fc + 1) * 128],
                                             S["phiq"][:, fc * 128:(fc + 1) * 128], S["identf"][:]))
        for fc in range(8):
            B.T(lambda t, fc=fc: t.transpose(P[1][:, fc * 128:(fc + 1) * 128],
                                             S["phik"][:, fc * 128:(fc + 1) * 128], S["identf"][:]))
        tphT = B.c["t"]
        B.W("vector", "t", tphT)
        B.W("scalar", "t", tphT)
        for hl in range(4):
            for hf in range(2):
                mb_ap = B.ap("hmask", [[1, 128], [0, 512]], off=hl)
                B.V(lambda v, hl=hl, hf=hf, mb_ap=mb_ap: v.tensor_tensor(
                    S["phiqm"][:, hl * 1024 + hf * 512:hl * 1024 + hf * 512 + 512],
                    P[0][:, hf * 512:hf * 512 + 512], mb_ap, ALU.mult))
        phfv = B.V(lambda v: v.tensor_copy(S["phikf"][:], P[1][:, 0:1024]))

        if STAGE < 7:
            marks[("cidone", ci)] = B.c["v"]
            return
        B.W("vector", "s", sip)
        for m in range(M):
            ipb = B.ap("pip", [[1, 128], [2, 32], [0, 32]], off=m)
            B.V(lambda v, m=m, ipb=ipb: v.tensor_tensor(S["ks"][:, m * H:(m + 1) * H], S["phik"][:], ipb, ALU.mult))
            vsb = B.ap("vs", [[1, 128], [66, 32], [1, 32]], off=m * 33)
            vsr = B.ap("vstg", [[1, 128], [33, 32], [1, 32]], off=0)
            B.V(lambda v, vsb=vsb, vsr=vsr, ipb=ipb: v.tensor_tensor(vsb, vsr, ipb, ALU.mult))
            ipd = B.ap("vs", [[1, 128], [66, 32], [1, 1]], off=m * 33 + 32)
            ips = B.ap("pip", [[1, 128], [2, 32], [1, 1]], off=m)
            B.V(lambda v, ipd=ipd, ips=ips: v.tensor_copy(ipd, ips))
        opsv = B.c["v"]

        if STAGE < 8:
            marks[("cidone", ci)] = B.c["v"]
            return
        B.W("tensor", "v", opsv)
        B.W("tensor", "s", pslot)
        prevmarks = {}
        for g8 in range(8):
            if g8 > 0:
                B.W("tensor", "v", prevmarks["gev"])
            for hl in range(4):
                B.T(lambda t, g8=g8, hl=hl: t.matmul(
                    P[4][:, hl * 128:(hl + 1) * 128],
                    S["phikf"][:, g8 * 128:(g8 + 1) * 128],
                    S["phiqm"][:, hl * 1024 + g8 * 128:hl * 1024 + (g8 + 1) * 128],
                    start=True, stop=True))
            tg = B.c["t"]
            B.W("vector", "t", tg)
            gev = B.V(lambda v: v.tensor_tensor(S["gsb"][:], P[4][:, 0:512], S["tmask4"][:], ALU.mult))
            prevmarks["gev"] = gev
            B.W("tensor", "v", gev)
            if g8 > 0:
                B.W("tensor", "v", prevmarks["cmb"])
            for hl in range(4):
                h = 4 * g8 + hl
                if SUB < 3:
                    continue
                B.T(lambda t, hl=hl, h=h: t.matmul(
                    P[5][:, hl * 66:hl * 66 + 66],
                    S["gsb"][:, hl * 128:(hl + 1) * 128],
                    S["vs"][:, h * 66:(h + 1) * 66],
                    start=True, stop=(SUB < 4)))
                if SUB < 4:
                    continue
                for m in range(M):
                    soff = (b * 16 + g8 * 2 + m) * 132
                    B.T(lambda t, hl=hl, m=m, soff=soff, g8=g8: t.matmul(
                        P[5][:, hl * 66 + m * 33:hl * 66 + m * 33 + 33],
                        S["phiqm"][:, hl * 1024 + g8 * 128:hl * 1024 + (g8 + 1) * 128],
                        S["Sbf"][:, soff + 33 * hl:soff + 33 * hl + 33],
                        start=False, stop=(m == M - 1)))
            tav = B.c["t"]
            if g8 > 0:
                B.W("tensor", "v", prevmarks["stv"])
            for m in range(M):
                if SUB < 5:
                    continue
                B.T(lambda t, g8=g8, m=m: t.matmul(
                    P[2][:, m * 132:(m + 1) * 132],
                    S["ks"][:, m * H + g8 * 128:m * H + (g8 + 1) * 128],
                    S["vstg"][:, 4 * g8 * 33:4 * g8 * 33 + 132],
                    start=True, stop=True))
            tst = B.c["t"]
            if SUB < 6:
                continue

            B.W("vector", "t", tav)
            B.W("vector", "s", pslot)
            dra = B.app(5, [[1, 128], [66, 4], [33, 2]], off=32)
            ppg = B.ap("pp", [[1, 128], [2, 4], [1, 2]], off=8 * g8)
            B.V(lambda v, dra=dra, ppg=ppg: v.tensor_tensor(S["mbstat"][:, 0:8], dra, ppg, ALU.mult))
            B.V(lambda v: v.tensor_scalar(S["mbstat"][:, 0:8], S["mbstat"][:, 0:8], DEN_EPS, None, ALU.add))
            B.V(lambda v: v.reciprocal(S["mbstat"][:, 8:16], S["mbstat"][:, 0:8]))
            mwg = B.ap("mixw", [[1, 128], [2, 4], [1, 2]], off=8 * g8)
            B.V(lambda v, mwg=mwg, ppg=ppg: v.tensor_tensor(S["mbstat"][:, 0:8], mwg, ppg, ALU.mult))
            B.V(lambda v: v.tensor_tensor(S["mbstat"][:, 0:8], S["mbstat"][:, 0:8], S["mbstat"][:, 8:16], ALU.mult))
            for m in range(M):
                numm = B.app(5, [[1, 128], [66, 4], [1, 32]], off=m * 33)
                wmb = B.ap("mbstat", [[1, 128], [2, 4], [0, 32]], off=m)
                dst = (S["scr1"] if m == 0 else S["scr2"])[:, 0:128]
                B.V(lambda v, numm=numm, wmb=wmb, dst=dst: v.tensor_tensor(dst, numm, wmb, ALU.mult))
            B.V(lambda v, g8=g8: v.tensor_tensor(S["yt"][:, g8 * 128:(g8 + 1) * 128],
                                                 S["scr1"][:, 0:128], S["scr2"][:, 0:128], ALU.add))
            prevmarks["cmb"] = B.c["v"]
            B.W("vector", "t", tst)
            pprev = ((ci + 1) % 2) * 16
            for m in range(M):
                soff = (b * 16 + g8 * 2 + m) * 132
                pcp = S["plastc"][:, pprev + 2 * g8 + m:pprev + 2 * g8 + m + 1]
                pcc = S["plastc"][:, slot + 2 * g8 + m:slot + 2 * g8 + m + 1]
                B.V(lambda v, soff=soff, pcp=pcp, m=m: v.scalar_tensor_tensor(
                    S["Tma"][:, soff:soff + 132], S["Tma"][:, soff:soff + 132], pcp,
                    P[2][:, m * 132:(m + 1) * 132], ALU.mult, ALU.add))
                B.V(lambda v, soff=soff, pcc=pcc: v.tensor_scalar(
                    S["Sbf"][:, soff:soff + 132], S["Tma"][:, soff:soff + 132], pcc, None, ALU.mult))
            stv = B.c["v"]
            prevmarks["stv"] = stv
        yv_all = B.c["v"]
        if DEBUG and ci == 0:
            srcs2 = [("phiqm", 0, 128), ("phiqm", 1024, 128), ("gsb", 0, 128),
                     ("vs", 0, 66), ("Sbf", (b * 16) * 132, 132), ("ks", 0, 128)]
            for i6, (nm, off, wd) in enumerate(srcs2):
                B.W("vector", "ddb", B.c["ddb"])
                dv_ = B.V(lambda v, nm=nm, off=off, wd=wd: v.tensor_copy(S["dbg"][:, 0:wd], S[nm][:, off:off + wd]))
                B.W("sync", "v", dv_)
                B.D("ddb", lambda e, i6=i6, wd=wd: e.dma_start(dr["d_s8"][i6][:, 0:wd], S["dbg"][:, 0:wd]))
        if DEBUG:
            B.W("sync", "v", yv_all)
            B.D("ddb", lambda e: e.dma_start(dr["d_y"][ci], S["yt"][:]))

        if STAGE < 9:
            marks[("cidone", ci)] = B.c["v"]
            return
        B.W("tensor", "v", yv_all)
        for fc in range(8):
            B.T(lambda t, fc=fc: t.transpose(P[0][:, fc * 128:(fc + 1) * 128],
                                             S["yt"][:, fc * 128:(fc + 1) * 128], S["identf"][:]))
        tyT = B.c["t"]
        B.W("vector", "t", tyT)
        yfv = B.V(lambda v: v.tensor_copy(S["yfm"][:], P[0][:, 0:1024]))
        B.W("tensor", "v", yfv)
        for k in range(8):
            for oc in (0, 512):
                B.T(lambda t, oc=oc, k=k: t.matmul(P[1][:, oc:oc + 512],
                                                   S["yfm"][:, k * 128:(k + 1) * 128],
                                                   S["wo"][:, k * H + oc:k * H + oc + 512],
                                                   start=(k == 0), stop=False))
        for oc in (0, 512):
            B.T(lambda t, oc=oc: t.matmul(P[1][:, oc:oc + 512], S["onesrow"][:],
                                          S["bout_row"][:, oc:oc + 512], start=False, stop=True))
        two = B.c["t"]
        B.W("vector", "t", two)
        zv = B.V(lambda v: v.tensor_tensor(S["zt"][:], P[1][:, 0:1024], S["xln"][:], ALU.add))
        B.W("scalar", "v", zv)
        nzv = ln_stats_apply("zt", "scr1")
        B.V(lambda v: v.tensor_tensor(S["scr2"][:], S["scr1"][:], S["lngbc"][:], ALU.mult))
        B.V(lambda v: v.tensor_tensor(S["scr1"][:], S["scr2"][:], S["lnbbc"][:], ALU.add))
        x1v = B.V(lambda v: v.tensor_tensor(S["xt"][:], S["xt"][:], S["scr1"][:], ALU.add))
        B.W("sync", "v", x1v)
        B.D("dxo", lambda e: e.dma_start(dr["x1_s"][b, t0:t0 + C, :], S["xt"][:]))
        if PHASES == "attn":
            B.D("dou", lambda e: e.dma_start(dr["out"][b, t0:t0 + C, :], S["xt"][:]))
        if DEBUG:
            B.D("ddb", lambda e: e.dma_start(dr["d_x1"][ci], S["xt"][:]))

        if STAGE < 10:
            marks[("cidone", ci)] = B.c["v"]
            return
        B.W("scalar", "v", x1v)
        z2v = ln_stats_apply("xt", "xln")
        B.W("tensor", "v", z2v)
        ex2 = fm_transpose("xln", "xlnfm")
        B.W("sync", "v", ex2)
        for fc in range(8):
            B.D("dzo", lambda e, fc=fc: e.dma_start(dr["zlns"][b, fc, :, t0:t0 + C],
                                                    S["xlnfm"][:, fc * 128:(fc + 1) * 128]))
        B.W("tensor", "v", ex2)
        for k in range(8):
            for ho in range(8):
                B.T(lambda t, ho=ho, k=k: t.matmul(P[0][:, ho * 128:(ho + 1) * 128],
                                                   S["gw1"][:, k * H + ho * 128:k * H + ho * 128 + 128],
                                                   S["xlnfm"][:, k * 128:(k + 1) * 128],
                                                   start=(k == 0), stop=(k == 7)))
        tg1 = B.c["t"]
        B.W("scalar", "t", tg1)
        for ho in range(8):
            B.Sc(lambda s, ho=ho: s.activation(S["hg"][:, ho * 128:(ho + 1) * 128],
                                               P[0][:, ho * 128:(ho + 1) * 128], AF.Relu,
                                               bias=S["gb1c"][:, ho:ho + 1]))
        sg1 = B.c["s"]
        if DEBUG and ci in (0, 16):
            B.W("sync", "s", sg1)
            B.D("ddb", lambda e, ci=ci: e.dma_start(dr["d_hg"][ci // 16], S["hg"][:]))
        B.W("tensor", "s", sg1)
        for k in range(8):
            B.T(lambda t, k=k: t.matmul(P[4][0:8, 256:384], S["gw2"][:, k * 8:(k + 1) * 8],
                                        S["hg"][:, k * 128:(k + 1) * 128],
                                        start=(k == 0), stop=(k == 7)))
        tg2 = B.c["t"]
        B.W("vector", "t", tg2)
        B.V(lambda v, b=b: v.tensor_tensor(S["gacc"][:, b * 128:(b + 1) * 128],
                                           S["gacc"][:, b * 128:(b + 1) * 128],
                                           P[4][0:8, 256:384], ALU.add))
        if DEBUG:
            dgc = B.V(lambda v: v.tensor_copy(S["scrg"][0:8, :], P[4][0:8, 256:384]))
            B.W("sync", "v", dgc)
            B.D("ddb", lambda e, ci=ci: e.dma_start(dr["d_gacc"][ci], S["scrg"][0:8, :]))
        marks[("cidone", ci)] = B.c["v"]

    def plan_ffn():
        ae = marks["attn_end"]
        for e in ("tensor", "vector", "scalar", "gpsimd", "sync"):
            B.W(e, "t", ae["t"]); B.W(e, "v", ae["v"]); B.W(e, "s", ae["s"])
            for k in Bld.DKEYS:
                B.W(e, k, ae[k])

        dtb2 = B.D("dc", lambda e: e.dma_start(S["xtf"][0:8, :], dr["tb2"]))
        db2r = B.D("dc", lambda e: e.dma_start(S["scrF"][0:1, :], dr["b2row"]))
        for b in range(NB):
            B.V(lambda v, b=b: v.tensor_reduce(S["gcol"][:, b:b + 1], S["gacc"][:, b * 128:(b + 1) * 128],
                                               AX, ALU.add))
            B.V(lambda v, b=b: v.tensor_scalar(S["gcol"][:, b:b + 1], S["gcol"][:, b:b + 1],
                                               1.0 / T, S["gb2c"][:], ALU.mult, ALU.add))
        gv = B.c["v"]
        if DEBUG:
            B.W("sync", "v", gv)
            B.D("ddb", lambda e: e.dma_start(dr["d_g"], S["gcol"][:]))
        if FSUB < 2:
            return
        B.W("tensor", "v", gv)
        for b in range(NB):
            B.T(lambda t, b=b: t.transpose(P[4][0:1, 240 + 8 * b:248 + 8 * b],
                                           S["gcol"][:, b:b + 1], S["identf"][0:8, 0:8]))
        tgr = B.c["t"]
        B.W("vector", "t", tgr)
        grv = B.V(lambda v: v.tensor_copy(S["grow"][0:1, 0:16], P[4][0:1, 240:256]))
        B.W("tensor", "v", grv)
        tgb = B.T(lambda t: t.matmul(P[4][:, 0:16], S["onesrow"][:], S["grow"][0:1, 0:16], start=True, stop=True))
        B.W("vector", "t", tgb)
        gbv = B.V(lambda v: v.tensor_copy(S["gbc"][:], P[4][:, 0:16]))
        if FSUB < 3:
            return
        B.W("tensor", "v", gbv)
        for fcch in range(32):
            B.W("sync", "t", marks.get(("tb1t", fcch - 1), 0))
            B.D("dtb", lambda e, fcch=fcch: e.dma_start(S["tb1st"][:], dr["tb1"][:, fcch * 128:(fcch + 1) * 128]))
            dmb = B.c["dtb"]
            B.W("tensor", "dtb", dmb)
            marks[("tb1t", fcch)] = B.T(lambda t, fcch=fcch: t.matmul(
                P[5][:, fcch * 2:fcch * 2 + 2], S["tb1st"][:], S["gcol"][:], start=True, stop=True))
        tmb1 = B.c["t"]
        B.W("vector", "t", tmb1)
        b1b = B.ap("b1c", [[1, 128], [1, 32], [0, 2]], off=0)
        B.V(lambda v, b1b=b1b: v.tensor_tensor(S["mb1c"][:], P[5][:, 0:64], b1b, ALU.add))
        if DEBUG:
            dmb1 = B.c["v"]
            B.W("sync", "v", dmb1)
            B.D("ddb", lambda e: e.dma_start(dr["d_mb1"], S["mb1c"][:]))
        if FSUB < 4:
            return
        B.W("tensor", "dc", db2r)
        B.W("vector", "dc", db2r)
        for b in range(NB):
            for half in range(2):
                B.W("tensor", "v", B.c["v"])
                tm2 = B.T(lambda t, b=b, half=half: t.matmul(
                    P[2][0:1, 0:512], S["gcol"][:, b:b + 1],
                    S["xtf"][0:8, half * 512:(half + 1) * 512], start=True, stop=True))
                B.W("vector", "t", tm2)
                B.V(lambda v, b=b, half=half: v.tensor_tensor(
                    S["mb2row"][0:1, b * H + half * 512:b * H + half * 512 + 512],
                    P[2][0:1, 0:512],
                    S["scrF"][0:1, half * 512:(half + 1) * 512], ALU.add))
        mb2v = B.c["v"]

        if FSTAGE < 2:
            return
        NSLOT = 2
        stage_seq = []
        for mat in (1, 2):
            nhc, nfb = (8, 8) if mat == 1 else (32, 2)
            for hc in range(nhc):
                for fb in range(nfb):
                    stage_seq.append((mat, hc, fb))
        vdone = {}
        for si, (mat, hc, fb) in enumerate(stage_seq):
            wT, twT, dramb1 = ("w1T", "tw1T", "mw1b1") if mat == 1 else ("w2T", "tw2T", "mw2b1")
            slot = si % NSLOT
            B.W("gpsimd", "v", vdone.get(si - NSLOT, mb2v))
            B.DG(lambda e, wT=wT, hc=hc, fb=fb, slot=slot:
                 e.dma_start(S["wst"][:, slot * 512:slot * 512 + 512],
                             dr[wT][hc * 128:(hc + 1) * 128, fb * 512:(fb + 1) * 512]))
            for e8 in range(E):
                B.DG(lambda e, twT=twT, e8=e8, hc=hc, fb=fb, slot=slot:
                     e.dma_start(S["twst"][:, (slot * E + e8) * 512:(slot * E + e8) * 512 + 512],
                                 dr[twT][e8, hc * 128:(hc + 1) * 128, fb * 512:(fb + 1) * 512]))
            dall = B.c["dg"]
            B.W("vector", "dg", dall)
            for b in range(NB):
                if b == 0:
                    dsl = (S["mw1"][:, hc * F + fb * 512:hc * F + fb * 512 + 512] if mat == 1
                           else S["mw2"][:, hc * H + fb * 512:hc * H + fb * 512 + 512])
                else:
                    dsl = S["scrW"][:, 0:512]
                    B.W("vector", "dms", B.c["dms"])
                for e8 in range(E):
                    src1 = S["wst"][:, slot * 512:slot * 512 + 512] if e8 == 0 else dsl
                    B.V(lambda v, dsl=dsl, e8=e8, b=b, slot=slot, src1=src1:
                        v.scalar_tensor_tensor(dsl,
                                               S["twst"][:, (slot * E + e8) * 512:(slot * E + e8) * 512 + 512],
                                               S["gbc"][:, b * 8 + e8:b * 8 + e8 + 1],
                                               src1, ALU.mult, ALU.add))
                if b == 1:
                    vw = B.c["v"]
                    B.W("sync", "v", vw)
                    B.D("dms", lambda e, hc=hc, fb=fb, dramb1=dramb1:
                        e.dma_start(dr[dramb1][hc, :, fb * 512:(fb + 1) * 512], S["scrW"][:, 0:512]))
            vdone[si] = B.c["v"]
        f0_v = B.c["v"]
        marks["dms_total"] = B.c["dms"]

        if FSTAGE < 3:
            return
        mb1v_total = B.c["v"]
        B.W("scalar", "v", mb1v_total)
        for b in range(NB):
            if b == 1:
                B.W("sync", "t", marks[("f1tend", 0)])
                B.W("sync", "dms", marks["dms_total"])
                for hc in range(8):
                    B.D("dmw", lambda e, hc=hc: e.dma_start(S["mw1"][:, hc * F:(hc + 1) * F], dr["mw1b1"][hc]))
                for fcch in range(32):
                    B.D("dmw", lambda e, fcch=fcch: e.dma_start(S["mw2"][:, fcch * H:(fcch + 1) * H], dr["mw2b1"][fcch]))
                mwdx = B.c["dmw"]
                B.W("tensor", "dmw", mwdx)
            B.W("tensor", "v", mb2v if b == 0 else marks[("fbdone", 0)])
            for half in range(2):
                B.T(lambda t, b=b, half=half: t.matmul(
                    P[4][:, 0:512] if half == 0 else P[5][:, 0:512], S["onesrow"][:],
                    S["mb2row"][0:1, b * H + half * 512:b * H + half * 512 + 512], start=True, stop=True))
            tmb2 = B.c["t"]
            B.W("vector", "t", tmb2)
            B.V(lambda v: v.tensor_copy(S["mb2bc"][:, 0:512], P[4][:, 0:512]))
            mbbv = B.V(lambda v: v.tensor_copy(S["mb2bc"][:, 512:1024], P[5][:, 0:512]))

            for w in range(NW):
                wt0 = w * WIN
                B.W("sync", "t", marks.get(("f1t", b, w - 1), 0))
                for fcch in range(8):
                    B.D("dzr", lambda e, b=b, fcch=fcch, wt0=wt0:
                        e.dma_start(S["zrh"][:, fcch * WIN:(fcch + 1) * WIN],
                                    dr["zlns"][b, fcch, :, wt0:wt0 + WIN]))
                dzl = B.c["dzr"]
                B.W("tensor", "dzr", dzl)
                if b == 0 and w == 0:
                    B.W("tensor", "v", f0_v)
                B.W("scalar", "t", marks.get(("f2t", b, w - 1), 0))
                for fo in range(32):
                    pi = fo % 2
                    B.W("tensor", "s", marks.get(("gelu", b, w, fo - 2), 0))
                    for k in range(8):
                        B.T(lambda t, fo=fo, k=k, pi=pi: t.matmul(
                            P[pi][:, 0:WIN],
                            S["mw1"][:, k * F + fo * 128:k * F + fo * 128 + 128],
                            S["zrh"][:, k * WIN:(k + 1) * WIN],
                            start=(k == 0), stop=(k == 7)))
                    tf1 = B.c["t"]
                    B.W("scalar", "t", tf1)
                    sg = B.Sc(lambda s, fo=fo, pi=pi, b=b: s.activation(
                        S["u"][:, fo * WIN:(fo + 1) * WIN], P[pi][:, 0:WIN], AF.Gelu_apprx_tanh,
                        bias=S["mb1c"][:, fo * 2 + b:fo * 2 + b + 1]))
                    marks[("gelu", b, w, fo)] = sg
                marks[("f1t", b, w)] = B.c["t"]
                if DEBUG and b == 0 and w == 0:
                    B.W("sync", "s", marks[("gelu", 0, 0, 3)])
                    for fo in range(4):
                        B.D("ddb", lambda e, fo=fo: e.dma_start(dr["d_u"][fo], S["u"][:, fo * WIN:(fo + 1) * WIN]))
                for tt in range(WIN // 128):
                    dx1 = B.D("dxf", lambda e, b=b, wt0=wt0, tt=tt:
                              e.dma_start(S["xtf"][:], dr["x1_s"][b, wt0 + tt * 128:wt0 + tt * 128 + 128, :]))
                    B.W("tensor", "s", marks[("gelu", b, w, 31)])
                    B.W("tensor", "v", marks.get(("f2o", b, w, tt - 1), 0))
                    for k in range(32):
                        for ho in range(2):
                            B.T(lambda t, tt=tt, ho=ho, k=k: t.matmul(
                                P[2 + ho][:, 0:512],
                                S["u"][:, k * WIN + tt * 128:k * WIN + tt * 128 + 128],
                                S["mw2"][:, k * H + ho * 512:k * H + ho * 512 + 512],
                                start=(k == 0), stop=(k == 31)))
                    tf2 = B.c["t"]
                    B.W("vector", "t", tf2)
                    B.W("vector", "dxf", dx1)
                    B.W("vector", "v", mbbv)
                    B.W("vector", "dou", B.c["dou"])
                    B.V(lambda v: v.tensor_tensor(S["scrF"][:, 0:512], P[2][:, 0:512], S["xtf"][:, 0:512], ALU.add))
                    B.V(lambda v: v.tensor_tensor(S["scrF"][:, 512:1024], P[3][:, 0:512],
                                                  S["xtf"][:, 512:1024], ALU.add))
                    fv = B.V(lambda v: v.tensor_tensor(S["scrF"][:], S["scrF"][:], S["mb2bc"][:], ALU.add))
                    B.W("sync", "v", fv)
                    B.D("dou", lambda e, b=b, wt0=wt0, tt=tt: e.dma_start(
                        dr["out"][b, wt0 + tt * 128:wt0 + tt * 128 + 128, :], S["scrF"][:]))
                    marks[("f2o", b, w, tt)] = B.c["v"]
                marks[("f2t", b, w)] = B.c["t"]
            marks[("f1tend", b)] = B.c["t"]
            marks[("fbdone", b)] = B.c["v"]
        for k in Bld.DKEYS:
            if k != "dg":
                B.W("sync", k, B.c[k])
        B.W("gpsimd", "dg", B.c["dg"])
    def plan_ffn_outer():
        plan_ffn()
        if FSTAGE < 99:
            for e in ("tensor", "vector", "scalar", "gpsimd", "sync"):
                B.W(e, "t", B.c["t"]); B.W(e, "v", B.c["v"]); B.W(e, "s", B.c["s"])
                for k in Bld.DKEYS:
                    B.W(e, k, B.c[k])

    def run_pass(eng, eb, which):
        B.eng, B.eb = eng, eb
        if which == "attn":
            B.reset()
            plan_attn()
            marks["snap_attn"] = dict(B.c)
        else:
            B.reset(marks["snap_attn"])
            if PHASES == "all":
                plan_ffn_outer()
            else:
                for e in ("tensor", "vector", "scalar", "gpsimd", "sync"):
                    ae = marks["attn_end"]
                    B.W(e, "t", ae["t"]); B.W(e, "v", ae["v"]); B.W(e, "s", ae["s"])
                    for k in Bld.DKEYS:
                        B.W(e, k, ae[k])

    with contextlib.ExitStack() as s0:
        for nm, sh, dt in (
            ("identf", (128, 128), F32), ("ltri", (128, 128), F32),
            ("tmask4", (128, 512), BF16), ("onescol", (128, 1), F32),
            ("onesrow", (1, 128), F32), ("econst", (64, 128), F32),
            ("msel", (64, 16), F32), ("hmask", (128, 64), F32),
            ("cos_t", (128, NCHUNK * 16), F32),
            ("sin_t", (128, NCHUNK * 16), F32), ("bbm_row", (1, 128), F32),
            ("bout_row", (1, H), F32), ("lngbc", (128, H), F32),
            ("lnbbc", (128, H), F32), ("gb1c", (128, 8), F32),
            ("gb2c", (8, 1), F32), ("b1c", (128, 32), F32),
            ("stat", (128, 16), F32), ("mbstat", (128, 16), F32),
            ("gacc", (8, NB * 128), F32), ("gcol", (8, NB), F32),
            ("grow", (1, 16), F32), ("gbc", (128, 8 * NB), F32),
            ("mb1c", (128, 64), F32), ("mb2row", (1, NB * H), F32),
            ("scrg", (8, 128), F32),
        ):
            B.sb(s0, nm, sh, dt)
        if DEBUG:
            B.sb(s0, "dbg", (128, 140), F32)
        for pi_, wdt in enumerate((1024, 1024, 512, 512, 512, 512)):
            P.append(s0.enter_context(nc.psum_tensor("ps%d" % pi_, [128, wdt], F32)))
        for n in ("t", "v", "s", "g") + Bld.DKEYS:
            B.sem[n] = s0.enter_context(nc.semaphore("sem_" + n))

        with contextlib.ExitStack() as sA:
            for nm, sh, dt in (
                ("wq", (128, 8 * H), BF16), ("wk", (128, 8 * H), BF16),
                ("wv", (128, 8 * H), BF16), ("wo", (128, 8 * H), BF16),
                ("wbm", (128, 8 * 128), BF16), ("gw1", (128, 8 * H), BF16),
                ("gw2", (128, 8 * 8), BF16),
                ("xt", (128, H), F32), ("xln", (128, H), F32),
                ("xlnfm", (128, 8 * 128), BF16),
                ("rotq", (128, H), F32), ("rotk", (128, H), F32),
                ("phiq", (128, H), F32), ("phik", (128, H), F32),
                ("scr1", (128, H), F32), ("scr2", (128, H), F32),
                ("phiqm", (128, 4 * 8 * 128), BF16), ("phikf", (128, 8 * 128), BF16),
                ("vstg", (128, NH * 33), BF16), ("vs", (128, NH * 66), BF16),
                ("ks", (128, M * H), BF16), ("bmt", (128, 128), F32),
                ("pip", (128, 64), F32), ("pp", (128, 64), F32),
                ("mixw", (128, 64), F32), ("plastc", (128, 32), F32),
                ("gsb", (128, 4 * 128), BF16),
                ("Tma", (128, NB * 16 * 132), F32), ("Sbf", (128, NB * 16 * 132), BF16),
                ("yt", (128, H), F32), ("yfm", (128, 8 * 128), BF16),
                ("zt", (128, H), F32), ("hg", (128, 8 * 128), BF16),
            ):
                B.sb(sA, nm, sh, dt)
            with nc.Block() as blkA:
                @blkA.tensor
                def _(t):
                    run_pass("tensor", t, "attn")

                @blkA.vector
                def _(v):
                    run_pass("vector", v, "attn")

                @blkA.scalar
                def _(s):
                    run_pass("scalar", s, "attn")

                @blkA.gpsimd
                def _(g):
                    run_pass("gpsimd", g, "attn")

                @blkA.sync
                def _(sy):
                    run_pass("sync", sy, "attn")

        with contextlib.ExitStack() as sB:
            for nm, sh, dt in (
                ("mw1", (128, 8 * F), BF16), ("mw2", (128, 32 * H), BF16),
                ("u", (128, 32 * WIN), BF16), ("zrh", (128, 8 * WIN), BF16),
                ("wst", (128, 2 * 512), BF16), ("twst", (128, 2 * E * 512), BF16),
                ("scrW", (128, 512), BF16), ("xtf", (128, H), F32),
                ("scrF", (128, H), F32), ("mb2bc", (128, H), F32),
                ("tb1st", (8, 128), F32),
            ):
                B.sb(sB, nm, sh, dt)
            with nc.Block() as blkB:
                @blkB.tensor
                def _(t):
                    run_pass("tensor", t, "ffn")

                @blkB.vector
                def _(v):
                    run_pass("vector", v, "ffn")

                @blkB.scalar
                def _(s):
                    run_pass("scalar", s, "ffn")

                @blkB.gpsimd
                def _(g):
                    run_pass("gpsimd", g, "ffn")

                @blkB.sync
                def _(sy):
                    run_pass("sync", sy, "ffn")

    nc.finalize()
    return nc


_NC_CACHE = None


def kernel(**inputs):
    global _NC_CACHE
    weights = make_weight_map(inputs)
    if _NC_CACHE is None:
        _NC_CACHE = build_program()
    x = np.asarray(inputs["L0_attn_x"])
    in_maps = []
    for c in range(8):
        m = dict(weights)
        m["x"] = np.ascontiguousarray(x[2 * c:2 * c + 2])
        in_maps.append(m)
    res = run_bass_kernel_spmd(_NC_CACHE, in_maps, core_ids=list(range(8)))
    return np.concatenate([r["out"] for r in res.results], axis=0)
